# version 02
# baseline (speedup 1.0000x reference)
import numpy as np
import ml_dtypes
import concourse.bass as bass
import concourse.mybir as mybir
from concourse.bass_utils import run_bass_kernel_spmd
from contextlib import ExitStack

F32 = mybir.dt.float32
BF16 = mybir.dt.bfloat16
ALU = mybir.AluOpType
ACT = mybir.ActivationFunctionType
AXX = mybir.AxisListType.X

NSEQ, L, D, H = 4, 2048, 1024, 4
NCH = 8
CS = 128
NCK = L // CS
F = 4096
NFC = 32
TT = 512
NTT = L // TT
NT128 = L // 128
TOK = NSEQ * L
MAGIC = 12582912.0
PAD = 32
PL = PAD + L

ENGINES = ("sync", "tensor", "vector", "scalar", "gpsimd")
ESEM = {"sync": "d", "tensor": "te", "vector": "v", "scalar": "s", "gpsimd": "g"}


class Buf:
    __slots__ = ("wr", "rds")

    def __init__(self):
        self.wr = None
        self.rds = []


def build(dbg=False):
    nc = bass.Bass("TRN2", target_bir_lowering=False, debug=False)

    x_in = nc.dram_tensor("x", [TOK, D], F32, kind="ExternalInput").ap()
    shapes = {
        "wqT": [D, D], "wkT": [D, D], "wvT": [D, D], "wbT": [D, H],
        "qcw": [128, 32], "kcw": [128, 32], "vcw": [128, 32],
        "firS": [128, 24], "firL": [128, 248],
        "fg1T": [1792, D], "fgb1": [128, 8], "fg2T": [D, H], "fgb2": [H, 1],
        "ltemp": [1, 1], "gnw": [128, 2], "woT": [D, D],
        "w1": [F, D], "b1": [128, 32], "w2": [D, F], "b2": [128, 8],
    }
    IN = {n: nc.dram_tensor(n, sh, F32, kind="ExternalInput").ap() for n, sh in shapes.items()}
    y_out = nc.dram_tensor("y", [TOK, D], F32, kind="ExternalOutput").ap()

    x2_scr = nc.dram_tensor("x2_scr", [TOK, D], F32,
                            kind="ExternalOutput" if dbg else "Internal").ap()
    h_scr = nc.dram_tensor("h_scr", [NFC, 128, TOK], F32,
                           kind="ExternalOutput" if dbg else "Internal").ap()
    dbg_d = nc.dram_tensor("dbg_d", [NCH * 128, TOK], F32,
                           kind="ExternalOutput").ap() if dbg else None
    dbg_w = nc.dram_tensor("dbg_w", [4, TOK], F32, kind="ExternalOutput").ap() if dbg else None
    dbg_o = nc.dram_tensor("dbg_o", [NCH * 128, TOK], F32, kind="ExternalOutput").ap() if dbg else None
    dbg_g = nc.dram_tensor("dbg_g", [128, TOK], F32, kind="ExternalOutput").ap() if dbg else None
    dbg_l = nc.dram_tensor("dbg_l", [4, TOK], F32, kind="ExternalOutput").ap() if dbg else None
    dbg_fs = nc.dram_tensor("dbg_fs", [128, TOK], F32, kind="ExternalOutput").ap() if dbg else None
    dbg_fl = nc.dram_tensor("dbg_fl", [128, TOK], F32, kind="ExternalOutput").ap() if dbg else None
    dbg_w4 = nc.dram_tensor("dbg_w4", [128, TOK], F32, kind="ExternalOutput").ap() if dbg else None
    cc_in = nc.dram_tensor("cc_in", [128, 128], F32, kind="Internal").ap()
    cc_out = nc.dram_tensor("cc_out", [128, 128], F32, kind="Internal").ap()

    consts = {
        "idb": np.eye(128, dtype=ml_dtypes.bfloat16),
        "idf": np.eye(128, dtype=np.float32),
        "mlo": np.tril(np.ones((128, 128), np.float32), -1).astype(ml_dtypes.bfloat16),
        "mup": np.triu(np.ones((128, 128), np.float32), 1).astype(ml_dtypes.bfloat16),
        "mupi": np.triu(np.ones((128, 128), np.float32), 0).astype(ml_dtypes.bfloat16),
        "onesb": np.ones((128, 128), ml_dtypes.bfloat16),
        "ones1f": np.ones((1, 128), np.float32),
        "ones1b": np.ones((1, 128), ml_dtypes.bfloat16),
        "magicc": np.full((128, 1), MAGIC, np.float32),
        "i1024": np.full((128, 1), 1.0 / 1024, np.float32),
        "eps5": np.full((128, 1), 1e-5, np.float32),
    }
    CIN = {n: nc.inline_tensor(v, name="c_" + n).ap() for n, v in consts.items()}

    semnames = ["d", "ds", "dt", "te", "v", "s", "g", "gd", "gs", "gw", "gz", "st"]
    steps = {e: [] for e in ENGINES}
    cnt = {s: 0 for s in semnames}
    BUFS = {}

    def buf(t):
        k = id(t)
        if k not in BUFS:
            BUFS[k] = Buf()
        return BUFS[k]

    MARK = {}

    def mark_of(t):
        return MARK.get(id(t), t)

    def step(eng, fn, reads=(), writes=(), extra=(), sem=None, amt=1):
        rb = [buf(mark_of(t)) for t in reads]
        wb = [buf(mark_of(t)) for t in writes]
        waits = []
        for b in rb:
            if b.wr is not None:
                waits.append(b.wr)
        for b in wb:
            waits += b.rds
            if b.wr is not None:
                waits.append(b.wr)
        waits += [w for w in extra if w is not None]
        if sem is None:
            sem = ESEM[eng]
        cnt[sem] += amt
        ev = (sem, cnt[sem])
        steps[eng].append((waits, fn, (sem, amt)))
        for b in rb:
            b.rds.append(ev)
        for b in wb:
            b.wr = ev
            b.rds = []
        return ev

    _last_dma = {}

    def dma(dst, src, reads=(), writes=(), eng="sync", extra=(), sem=None):
        if sem is None:
            sem = "gd" if eng == "gpsimd" else "d"
        ex = list(extra)
        if sem in _last_dma:
            ex.append(_last_dma[sem])
        ev = step(eng, lambda e: e.dma_start(out=dst, in_=src),
                  reads=reads, writes=writes, extra=ex, sem=sem, amt=16)
        _last_dma[sem] = ev
        return ev

    _last_dt = [None]

    def dmat(dst, src, reads=(), writes=(), extra=()):
        ex = list(extra)
        if _last_dt[0] is not None:
            ex.append(_last_dt[0])
        ev = step("sync", lambda e: e.dma_start_transpose(out=dst, in_=src),
                  reads=reads, writes=writes, extra=ex, sem="dt", amt=16)
        _last_dt[0] = ev
        return ev

    with ExitStack() as ctx:
        def sb(name, shape, dt):
            return ctx.enter_context(nc.sbuf_tensor(name, shape, dt))

        def region(arena, c0, c1, name=None):
            ap = arena[:, c0:c1]
            MARK[id(ap)] = arena
            return ap

        C = {n: sb("k_" + n, list(consts[n].shape),
                   BF16 if consts[n].dtype == ml_dtypes.bfloat16 else F32) for n in consts}
        WS = {}
        for n, sh in [("qcw", [128, 32]), ("kcw", [128, 32]), ("vcw", [128, 32]),
                      ("firS", [128, 24]), ("firL", [128, 248]), ("fgb1", [128, 8]),
                      ("fgb2", [4, 1]), ("gnw", [128, 2]), ("b1", [128, 32]), ("b2", [128, 8])]:
            WS[n] = sb("w_" + n, sh, F32)
        w_wbT = sb("w_wbT", [128, 32], BF16)
        w_fg2T = sb("w_fg2T", [128, 32], BF16)

        XLN = sb("XLN", [128, 16384], BF16)
        DLT = sb("DLT", [128, 16384], BF16)
        VFM = sb("VFM", [128, NCH * PL], BF16)
        W1A = sb("W1A", [128, 16384], BF16)
        AR = sb("AR", [128, 14336], BF16)
        KTA = sb("KTA", [128, 4160], BF16)

        kpos = region(W1A, 0, 4096)
        vpos = region(W1A, 4096, 8192)
        dpos0 = region(W1A, 8192, 10240)
        dpos1 = region(W1A, 10240, 12288)
        rt_s = region(W1A, 12288, 14336)
        at_s = region(W1A, 14336, 16384)
        fir_s = region(W1A, 0, 4096)
        fir_l = region(W1A, 4096, 8192)
        ghid_sb = region(W1A, 8192, 12288)
        onrm_sb = region(W1A, 12288, 16384)
        w1qA = W1A

        q_fm = region(AR, 0, 4160)
        k_fm = region(AR, 4160, 8320)
        wqk = region(AR, 8320, 12416)

        pre_c = region(KTA, 0, 2080)
        xq_fm = region(KTA, 0, 4096)
        w2qC = region(KTA, 0, 4096)

        qw_tile = region(XLN, 0, 4096)

        def regt(arena, p0, p1, c0, c1):
            ap = arena[p0:p1, c0:c1]
            MARK[id(ap)] = arena
            return ap

        s_sb = sb("s_sb", [128, 512], BF16)
        z_sb = sb("z_sb", [128, 256], BF16)
        u_sb = sb("u_sb", [128, 256], BF16)
        xch = [sb("xch0", [128, 128], F32), sb("xch1", [128, 128], F32)]
        bch = [sb("bch0", [128, 128], F32), sb("bch1", [128, 128], F32)]
        rch = [sb("rch0", [128, 128], F32), sb("rch1", [128, 128], F32)]
        sq_sb = sb("sq_sb", [128, 1024], BF16)
        bcb_sb = region(AR, 12416, 12928)
        bpos_sb = sb("bpos_sb", [128, 64], F32)
        itmp_sb = sb("itmp_sb", [128, 1], F32)
        sc1_sb = sb("sc1_sb", [1, 8], F32)
        w4b_sb = region(W1A, 8192, 10240)
        wr4_sb = sb("wr4_sb", [1, TT], F32)
        rms_sb = sb("rms_sb", [128, 512], F32)
        aofm_sb = sb("aofm_sb", [128, 4096], BF16)
        mean_sb = region(aofm_sb, 0, 3072)
        wo_st = [sb("wo_st0", [128, 1024], BF16)]
        xt_t = sb("xt_t", [128, 1024], F32)
        ao_tm = sb("ao_tm", [128, 1024], F32)
        lnsq = sb("lnsq", [128, 1024], F32)
        x2_t = lnsq
        wstg2 = ao_tm
        bt_f32 = regt(lnsq, 0, 4, 0, 512)
        stat_sb = sb("stat_sb", [128, 16], F32)
        m3_sb = sb("m3_sb", [128, 64], F32)
        r3_sb = sb("r3_sb", [128, 64], F32)
        gmax_sb = sb("gmax_sb", [128, 4], F32)
        ccst_sb = sb("ccst_sb", [128, 128], F32)
        sin_sb = sb("sin_sb", [128, 4], F32)
        sw1_sb = sb("sw1_sb", [128, 32], F32)
        a1_sb = sb("a1_sb", [128, 32], F32)
        bq1_sb = sb("bq1_sb", [128, 32], F32)
        sw2_sb = sb("sw2_sb", [128, 8], F32)
        a2_sb = sb("a2_sb", [128, 8], F32)
        bq2_sb = sb("bq2_sb", [128, 8], F32)
        tiny_sb = sb("tiny_sb", [128, 8], F32)
        tiny2_sb = sb("tiny2_sb", [128, 32], F32)
        xq_tm = sb("xq_tm", [128, 1024], BF16)
        ao_bf = xq_tm
        ht = [rms_sb, None]

        ht[1] = regt(xt_t, 0, 128, 0, 512)
        ps_S = ctx.enter_context(nc.psum_tensor("ps_S", [128, 512], F32))
        ps_a = ctx.enter_context(nc.psum_tensor("ps_a", [128, 512], F32))
        ps_o = ctx.enter_context(nc.psum_tensor("ps_o", [128, 512], F32))
        ps_mm = ctx.enter_context(nc.psum_tensor("ps_mm", [128, 512], F32))
        ps_mm2 = ctx.enter_context(nc.psum_tensor("ps_mm2", [128, 512], F32))
        ps_p1 = ctx.enter_context(nc.psum_tensor("ps_p1", [128, 512], F32))
        ps_p2 = ctx.enter_context(nc.psum_tensor("ps_p2", [128, 512], F32))
        ps_t = ctx.enter_context(nc.psum_tensor("ps_t", [128, 512], F32))

        sems = {n: ctx.enter_context(nc.semaphore(n)) for n in semnames}
        block = ctx.enter_context(nc.Block())
        MG = C["magicc"]

        def barrier():
            evs = [(s, cnt[s]) for s in semnames]
            for eng in ENGINES:
                fn = (lambda e: e.nop()) if eng != "sync" else (lambda e: e.nop())
                step(eng, fn, extra=evs)

        for n in consts:
            lowp = consts[n].shape[0] < 128
            dma(C[n][:, :], CIN[n][:, :], writes=[C[n]], sem="ds" if lowp else "d")
        for n in WS:
            dma(WS[n][:, :], IN[n][:, :], writes=[WS[n]], sem="ds" if n == "fgb2" else "d")
        dma(w_wbT[:, :].rearrange("p (c h) -> p c h", h=4), IN["wbT"].rearrange("(c p) h -> p c h", p=128), writes=[w_wbT], eng="gpsimd")
        dma(w_fg2T[:, :].rearrange("p (c h) -> p c h", h=4), IN["fg2T"].rearrange("(c p) h -> p c h", p=128), writes=[w_fg2T], eng="gpsimd")

        dma(sc1_sb[0:1, 0:1], IN["ltemp"][:, :], writes=[sc1_sb], sem="ds")
        step("scalar", lambda e: e.activation(out=sc1_sb[0:1, 1:2], in_=sc1_sb[0:1, 0:1], func=ACT.Exp),
             reads=[sc1_sb], writes=[sc1_sb])
        step("scalar", lambda e: e.activation(out=sc1_sb[0:1, 2:3], in_=sc1_sb[0:1, 1:2], func=ACT.Ln, bias=1.0),
             reads=[sc1_sb], writes=[sc1_sb])
        step("vector", lambda e: e.tensor_scalar(out=sc1_sb[0:1, 3:4], in0=sc1_sb[0:1, 2:3],
                                                 scalar1=1e-4, scalar2=None, op0=ALU.add),
             reads=[sc1_sb], writes=[sc1_sb])
        step("vector", lambda e: e.reciprocal(out=sc1_sb[0:1, 4:5], in_=sc1_sb[0:1, 3:4]),
             reads=[sc1_sb], writes=[sc1_sb])
        step("tensor", lambda e: e.matmul(ps_t[0:128, 0:1], C["ones1f"][0:1, :], sc1_sb[0:1, 4:5],
                                          start=True, stop=True),
             reads=[sc1_sb, C["ones1f"]], writes=[ps_t])
        step("scalar", lambda e: e.copy(out=itmp_sb[:, :], in_=ps_t[0:128, 0:1]),
             reads=[ps_t], writes=[itmp_sb])
        step("gpsimd", lambda e: e.memset(ccst_sb[:, :], 0.0), writes=[ccst_sb])
        step("gpsimd", lambda e: e.memset(gmax_sb[:, :], -1e30), writes=[gmax_sb])
        for c in range(NCH):
            step("gpsimd", lambda e, c=c: e.memset(VFM[:, c * PL:c * PL + PAD], 0.0), writes=[VFM])

        convg = sq_sb

        def conv(src, s0raw, dst, d0, wt, wc0, k, width):
            s0 = s0raw - (k - 1)
            def chain(eng, target, t0, taps):
                for jj, j in enumerate(taps):
                    so = s0 + j
                    if jj == 0:
                        step(eng, lambda e, so=so, j=j, target=target, t0=t0: e.tensor_scalar(
                            out=target[:, t0:t0 + width], in0=src[:, so:so + width],
                            scalar1=wt[:, wc0 + j:wc0 + j + 1], scalar2=None, op0=ALU.mult),
                            reads=[src, wt], writes=[target])
                    else:
                        step(eng, lambda e, so=so, j=j, target=target, t0=t0: e.scalar_tensor_tensor(
                            out=target[:, t0:t0 + width], in0=src[:, so:so + width],
                            scalar=wt[:, wc0 + j:wc0 + j + 1], in1=target[:, t0:t0 + width],
                            op0=ALU.mult, op1=ALU.add),
                            reads=[src, wt, target], writes=[target])
            chain("vector", dst, d0, list(range(k)))

        def ln_stats_apply(src_tm, ctr_tm, add_tm, out_tm):
            step("vector", lambda e: e.reduce_sum(out=stat_sb[:, 0:1], in_=src_tm[:, :], axis=AXX),
                 reads=[src_tm], writes=[stat_sb])
            step("scalar", lambda e: e.mul(out=stat_sb[:, 1:2], in_=stat_sb[:, 0:1], mul=-1.0 / 1024),
                 reads=[stat_sb], writes=[stat_sb])
            step("vector", lambda e: e.tensor_scalar(out=ctr_tm[:, :], in0=src_tm[:, :],
                                                     scalar1=stat_sb[:, 1:2], scalar2=None, op0=ALU.add),
                 reads=[src_tm, stat_sb], writes=[ctr_tm])
            step("vector", lambda e: e.tensor_tensor(out=lnsq[:, :], in0=ctr_tm[:, :], in1=ctr_tm[:, :],
                                                     op=ALU.mult),
                 reads=[ctr_tm], writes=[lnsq])
            step("vector", lambda e: e.reduce_sum(out=stat_sb[:, 2:3], in_=lnsq[:, :], axis=AXX),
                 reads=[lnsq], writes=[stat_sb])
            step("scalar", lambda e: e.activation(out=stat_sb[:, 3:4], in_=stat_sb[:, 2:3], func=ACT.Sqrt,
                                                  scale=C["i1024"][:, :], bias=C["eps5"][:, :]),
                 reads=[stat_sb, C["i1024"], C["eps5"]], writes=[stat_sb])
            step("vector", lambda e: e.reciprocal(out=stat_sb[:, 4:5], in_=stat_sb[:, 3:4]),
                 reads=[stat_sb], writes=[stat_sb])
            step("scalar", lambda e: e.copy(out=stat_sb[:, 5:6], in_=stat_sb[:, 4:5]),
                 reads=[stat_sb], writes=[stat_sb])
            step("vector", lambda e: e.scalar_tensor_tensor(out=out_tm[:, :], in0=ctr_tm[:, :],
                                                            scalar=stat_sb[:, 5:6], in1=add_tm[:, :],
                                                            op0=ALU.mult, op1=ALU.add),
                 reads=[ctr_tm, stat_sb, add_tm], writes=[out_tm])

        barrier()
        for sq in range(NSEQ):
            tokb = sq * L
            barrier()

            for t in range(NT128):
                r0 = tokb + 128 * t
                dma(xt_t[:, :], x_in[r0:r0 + 128, :], writes=[xt_t])
                step("vector", lambda e: e.reduce_sum(out=stat_sb[:, 0:1], in_=xt_t[:, :], axis=AXX),
                     reads=[xt_t], writes=[stat_sb])
                step("scalar", lambda e: e.mul(out=stat_sb[:, 1:2], in_=stat_sb[:, 0:1], mul=-1.0 / 1024),
                     reads=[stat_sb], writes=[stat_sb])
                step("vector", lambda e: e.tensor_scalar(out=xt_t[:, :], in0=xt_t[:, :],
                                                         scalar1=stat_sb[:, 1:2], scalar2=None, op0=ALU.add),
                     reads=[xt_t, stat_sb], writes=[xt_t])
                step("vector", lambda e: e.tensor_tensor(out=lnsq[:, :], in0=xt_t[:, :], in1=xt_t[:, :],
                                                         op=ALU.mult),
                     reads=[xt_t], writes=[lnsq])
                step("vector", lambda e: e.reduce_sum(out=stat_sb[:, 2:3], in_=lnsq[:, :], axis=AXX),
                     reads=[lnsq], writes=[stat_sb])
                step("scalar", lambda e: e.activation(out=stat_sb[:, 3:4], in_=stat_sb[:, 2:3], func=ACT.Sqrt,
                                                      scale=C["i1024"][:, :], bias=C["eps5"][:, :]),
                     reads=[stat_sb, C["i1024"], C["eps5"]], writes=[stat_sb])
                step("vector", lambda e: e.reciprocal(out=stat_sb[:, 4:5], in_=stat_sb[:, 3:4]),
                     reads=[stat_sb], writes=[stat_sb])
                step("scalar", lambda e: e.copy(out=stat_sb[:, 5:6], in_=stat_sb[:, 4:5]),
                     reads=[stat_sb], writes=[stat_sb])
                step("vector", lambda e: e.tensor_scalar(out=xq_tm[:, :], in0=xt_t[:, :],
                                                          scalar1=stat_sb[:, 5:6], scalar2=None,
                                                          op0=ALU.mult),
                     reads=[xt_t, stat_sb], writes=[xq_tm])
                dmat(XLN[:, :].rearrange("p (c tk) -> p c tk", tk=L)[:, :, 128 * t:128 * t + 128],
                     xq_tm[:, :].rearrange("p (c f) -> p c f", f=128),
                     reads=[xq_tm], writes=[XLN])

            for tt in range(NTT):
                col = tt * TT
                def mmb(e, col=col):
                    ins = None
                    for kc in range(NCH):
                        ins = e.matmul(ps_t[0:4, 0:TT], w_wbT[:, 4 * kc:4 * kc + 4],
                                       XLN[:, kc * L + col:kc * L + col + TT],
                                       start=(kc == 0), stop=(kc == NCH - 1))
                    return ins
                step("tensor", mmb, reads=[XLN, w_wbT], writes=[ps_t])
                step("scalar", lambda e, col=col: e.activation(out=bt_f32[0:4, 0:TT], in_=ps_t[0:4, 0:TT],
                                                               func=ACT.Sigmoid),
                     reads=[ps_t], writes=[bt_f32])
                for bi in range(4):
                    t = 4 * tt + bi
                    step("tensor", lambda e, bi=bi: e.transpose(ps_t[0:128, 4:8],
                                                                bt_f32[0:4, 128 * bi:128 * bi + 128],
                                                                C["idf"][0:4, 0:4]),
                         reads=[bt_f32, C["idf"]], writes=[ps_t])
                    step("scalar", lambda e, t=t: e.copy(out=bpos_sb[:, 4 * t:4 * t + 4], in_=ps_t[0:128, 4:8]),
                         reads=[ps_t], writes=[bpos_sb])

            dma(AR[:, 0:8192].rearrange("p (kc m) -> p kc m", m=1024),
                IN["wvT"].rearrange("(kc p) m -> p kc m", p=128), writes=[AR], eng="gpsimd")
            for c in range(NCH):
                for tt in range(NTT):
                    col = tt * TT
                    pp = ps_mm if tt % 2 == 0 else ps_mm2
                    def mmv(e, c=c, col=col, pp=pp):
                        ins = None
                        for kc in range(NCH):
                            ins = e.matmul(pp[:, 0:TT], AR[:, kc * 1024 + 128 * c:kc * 1024 + 128 * c + 128],
                                           XLN[:, kc * L + col:kc * L + col + TT],
                                           start=(kc == 0), stop=(kc == NCH - 1))
                        return ins
                    step("tensor", mmv, reads=[XLN, AR], writes=[pp])
                    step("scalar", lambda e, col=col, pp=pp: e.copy(out=pre_c[:, PAD + col:PAD + col + TT],
                                                                    in_=pp[:, 0:TT]),
                         reads=[pp], writes=[pre_c])
                step("gpsimd", lambda e: e.memset(pre_c[:, 0:PAD], 0.0), writes=[pre_c])
                vo = c * PL + PAD
                conv(pre_c, PAD, VFM, vo, WS["vcw"], 4 * c, 4, L)
                step("scalar", lambda e, vo=vo: e.activation(out=VFM[:, vo:vo + L], in_=VFM[:, vo:vo + L],
                                                             func=ACT.Silu),
                     reads=[VFM], writes=[VFM])

            for h in range(H):
                wq_ap = IN["wqT"].rearrange("(kc p) m -> p kc m", p=128)[:, :, 256 * h:256 * h + 256]
                wk_ap = IN["wkT"].rearrange("(kc p) m -> p kc m", p=128)[:, :, 256 * h:256 * h + 256]
                dma(wqk[:, 0:2048].rearrange("p (kc m) -> p kc m", m=256), wq_ap, writes=[wqk], eng="gpsimd")
                dma(wqk[:, 2048:4096].rearrange("p (kc m) -> p kc m", m=256), wk_ap, writes=[wqk], eng="gpsimd")
                for base, dstt, cwn in ((0, q_fm, "qcw"), (2048, k_fm, "kcw")):
                    for cc in range(2):
                        for tt in range(NTT):
                            col = tt * TT
                            pp = ps_mm if tt % 2 == 0 else ps_mm2
                            def mmqk(e, base=base, cc=cc, col=col, pp=pp):
                                ins = None
                                for kc in range(NCH):
                                    ins = e.matmul(pp[:, 0:TT],
                                                   wqk[:, base + kc * 256 + 128 * cc:base + kc * 256 + 128 * cc + 128],
                                                   XLN[:, kc * L + col:kc * L + col + TT],
                                                   start=(kc == 0), stop=(kc == NCH - 1))
                                return ins
                            step("tensor", mmqk, reads=[XLN, wqk], writes=[pp])
                            step("scalar", lambda e, col=col, pp=pp: e.copy(
                                out=pre_c[:, PAD + col:PAD + col + TT], in_=pp[:, 0:TT]),
                                reads=[pp], writes=[pre_c])
                        step("gpsimd", lambda e: e.memset(pre_c[:, 0:PAD], 0.0), writes=[pre_c])
                        o = cc * PL + PAD
                        conv(pre_c, PAD, dstt, o, WS[cwn], 4 * (2 * h + cc), 4, L)
                        step("scalar", lambda e, o=o, dstt=dstt: e.activation(
                            out=dstt[:, o:o + L], in_=dstt[:, o:o + L], func=ACT.Silu),
                            reads=[dstt], writes=[dstt])

                for src, isk in ((q_fm, False), (k_fm, True)):
                    for tt in range(NTT):
                        col = tt * TT
                        for cc in range(2):
                            step("vector", lambda e, src=src, cc=cc, col=col: e.tensor_tensor(
                                out=sq_sb[:, cc * 512:cc * 512 + TT],
                                in0=src[:, cc * PL + PAD + col:cc * PL + PAD + col + TT],
                                in1=src[:, cc * PL + PAD + col:cc * PL + PAD + col + TT], op=ALU.mult),
                                reads=[src], writes=[sq_sb])
                        def mmn(e):
                            e.matmul(ps_t[:, 0:TT], C["onesb"][:, :], sq_sb[:, 0:TT], start=True, stop=False)
                            return e.matmul(ps_t[:, 0:TT], C["onesb"][:, :], sq_sb[:, 512:512 + TT],
                                            start=False, stop=True)
                        step("tensor", mmn, reads=[sq_sb, C["onesb"]], writes=[ps_t])
                        step("vector", lambda e: e.tensor_scalar(out=rms_sb[:, 0:TT], in0=ps_t[:, 0:TT],
                                                                 scalar1=1e-12, scalar2=None, op0=ALU.add),
                             reads=[ps_t], writes=[rms_sb])
                        step("vector", lambda e: e.reciprocal(out=rms_sb[:, 0:TT], in_=rms_sb[:, 0:TT]),
                             reads=[rms_sb], writes=[rms_sb])
                        step("scalar", lambda e: e.activation(out=rms_sb[:, 0:TT], in_=rms_sb[:, 0:TT],
                                                              func=ACT.Sqrt),
                             reads=[rms_sb], writes=[rms_sb])
                        for cc in range(2):
                            so = cc * PL + PAD + col
                            step("vector", lambda e, src=src, so=so: e.tensor_tensor(
                                out=src[:, so:so + TT], in0=src[:, so:so + TT], in1=rms_sb[:, 0:TT],
                                op=ALU.mult),
                                reads=[src, rms_sb], writes=[src])

                for cc in range(2):
                    dmat(kpos.rearrange("p (j cf) -> p j cf", cf=256)[:, :, 128 * cc:128 * cc + 128],
                         k_fm[:, cc * PL + PAD:cc * PL + PAD + L].rearrange("p (j f) -> p j f", f=128),
                         reads=[k_fm], writes=[kpos])
                    dmat(vpos.rearrange("p (j cf) -> p j cf", cf=256)[:, :, 128 * cc:128 * cc + 128],
                         VFM[:, (2 * h + cc) * PL + PAD:(2 * h + cc) * PL + PAD + L].rearrange(
                             "p (j f) -> p j f", f=128),
                         reads=[VFM], writes=[vpos])

                for j in range(NCK):
                    t0 = 128 * j
                    def mmg(e, t0=t0):
                        e.matmul(ps_p1[:, 0:128], k_fm[:, PAD + t0:PAD + t0 + 128],
                                 k_fm[:, PAD + t0:PAD + t0 + 128], start=True, stop=False)
                        e.matmul(ps_p1[:, 0:128], k_fm[:, PL + PAD + t0:PL + PAD + t0 + 128],
                                 k_fm[:, PL + PAD + t0:PL + PAD + t0 + 128], start=False, stop=True)
                        e.matmul(ps_p1[:, 256:384], k_fm[:, PAD + t0:PAD + t0 + 128],
                                 q_fm[:, PAD + t0:PAD + t0 + 128], start=True, stop=False)
                        return e.matmul(ps_p1[:, 256:384], k_fm[:, PL + PAD + t0:PL + PAD + t0 + 128],
                                        q_fm[:, PL + PAD + t0:PL + PAD + t0 + 128], start=False, stop=True)
                    step("tensor", mmg, reads=[k_fm, q_fm], writes=[ps_p1])
                    def masks(e, j=j, h=h):
                        e.scalar_tensor_tensor(out=xch[0][:, :], in0=ps_p1[:, 0:128],
                                               scalar=bpos_sb[:, 4 * j + h:4 * j + h + 1],
                                               in1=C["mlo"][:, :], op0=ALU.mult, op1=ALU.mult)
                        return e.tensor_tensor(out=at_s[:, 128 * j:128 * j + 128], in0=ps_p1[:, 256:384],
                                               in1=C["mupi"][:, :], op=ALU.mult)
                    step("vector", masks, reads=[ps_p1, C["mlo"], C["mupi"], bpos_sb],
                         writes=[xch[0], at_s])
                    step("tensor", lambda e: e.transpose(ps_p2[:, 384:512], xch[0][:, :], C["idf"][:, :]),
                         reads=[xch[0], C["idf"]], writes=[ps_p2])
                    def b0r0(e):
                        e.tensor_copy(out=bch[0][:, :], in_=ps_p2[:, 384:512])
                        return e.tensor_tensor(out=rch[0][:, :], in0=C["idf"][:, :], in1=bch[0][:, :],
                                               op=ALU.subtract)
                    step("vector", b0r0, reads=[ps_p2, C["idf"]], writes=[bch[0], rch[0]])
                    for k in range(1, 7):
                        xp, bp, rp = xch[(k - 1) % 2], bch[(k - 1) % 2], rch[(k - 1) % 2]
                        xn, bn, rn = xch[k % 2], bch[k % 2], rch[k % 2]
                        last = (k == 6)
                        def mmxb(e, xp=xp, bp=bp, last=last):
                            ins = e.matmul(ps_p2[:, 0:128], bp[:, :], xp[:, :], start=True, stop=True)
                            if not last:
                                ins = e.matmul(ps_p2[:, 128:256], xp[:, :], bp[:, :], start=True, stop=True)
                            return ins
                        step("tensor", mmxb, reads=[xp, bp], writes=[ps_p2])
                        def evxb(e, xn=xn, bn=bn, last=last):
                            ins = e.tensor_copy(out=xn[:, :], in_=ps_p2[:, 0:128])
                            if not last:
                                ins = e.tensor_copy(out=bn[:, :], in_=ps_p2[:, 128:256])
                            return ins
                        step("vector", evxb, reads=[ps_p2], writes=[xn, bn])
                        def mmr(e, xn=xn, rp=rp):
                            e.matmul(ps_p2[:, 256:384], C["idf"][:, :], rp[:, :], start=True, stop=False)
                            return e.matmul(ps_p2[:, 256:384], xn[:, :], rp[:, :], start=False, stop=True)
                        step("tensor", mmr, reads=[xn, rp, C["idf"]], writes=[ps_p2])
                        if last:
                            step("vector", lambda e, j=j: e.tensor_copy(
                                out=rt_s[:, 128 * j:128 * j + 128], in_=ps_p2[:, 256:384]),
                                reads=[ps_p2], writes=[rt_s])
                        else:
                            step("vector", lambda e, rn=rn: e.tensor_copy(out=rn[:, :], in_=ps_p2[:, 256:384]),
                                 reads=[ps_p2], writes=[rn])

                step("gpsimd", lambda e: e.memset(s_sb[:, :], 0.0), writes=[s_sb])
                for j in range(NCK):
                    t0 = 128 * j
                    def mmz(e, t0=t0):
                        e.matmul(ps_a[:, 0:256], k_fm[:, PAD + t0:PAD + t0 + 128], s_sb[:, 0:256],
                                 start=True, stop=False)
                        return e.matmul(ps_a[:, 0:256], k_fm[:, PL + PAD + t0:PL + PAD + t0 + 128],
                                        s_sb[:, 256:512], start=False, stop=True)
                    step("tensor", mmz, reads=[k_fm, s_sb], writes=[ps_a])
                    def zfn(e, j=j, h=h):
                        e.tensor_tensor(out=z_sb[:, :], in0=vpos[:, 256 * j:256 * j + 256],
                                        in1=ps_a[:, 0:256], op=ALU.subtract)
                        return e.tensor_scalar(out=z_sb[:, :], in0=z_sb[:, :],
                                               scalar1=bpos_sb[:, 4 * j + h:4 * j + h + 1],
                                               scalar2=None, op0=ALU.mult)
                    step("vector", zfn, reads=[vpos, bpos_sb, ps_a], writes=[z_sb])
                    step("tensor", lambda e, j=j: e.matmul(ps_a[:, 256:512], rt_s[:, 128 * j:128 * j + 128],
                                                           z_sb[:, :], start=True, stop=True),
                         reads=[rt_s, z_sb], writes=[ps_a])
                    step("vector", lambda e: e.tensor_copy(out=u_sb[:, :], in_=ps_a[:, 256:512]),
                         reads=[ps_a], writes=[u_sb])
                    def mmo(e, t0=t0, j=j):
                        e.matmul(ps_o[:, 0:256], q_fm[:, PAD + t0:PAD + t0 + 128], s_sb[:, 0:256],
                                 start=True, stop=False)
                        e.matmul(ps_o[:, 0:256], q_fm[:, PL + PAD + t0:PL + PAD + t0 + 128], s_sb[:, 256:512],
                                 start=False, stop=False)
                        return e.matmul(ps_o[:, 0:256], at_s[:, 128 * j:128 * j + 128], u_sb[:, :],
                                        start=False, stop=True)
                    step("tensor", mmo, reads=[q_fm, s_sb, at_s, u_sb], writes=[ps_o])
                    def dpev(e, j=j):
                        e.tensor_copy(out=dpos0[:, 128 * j:128 * j + 128], in_=ps_o[:, 0:128])
                        return e.tensor_copy(out=dpos1[:, 128 * j:128 * j + 128], in_=ps_o[:, 128:256])
                    step("vector", dpev, reads=[ps_o], writes=[dpos0, dpos1])
                    def mms(e, j=j):
                        e.matmul(ps_S[:, 0:256], kpos[:, 256 * j:256 * j + 128], u_sb[:, :],
                                 start=(j == 0), stop=(j == NCK - 1))
                        return e.matmul(ps_S[:, 256:512], kpos[:, 256 * j + 128:256 * j + 256], u_sb[:, :],
                                        start=(j == 0), stop=(j == NCK - 1))
                    step("tensor", mms, reads=[kpos, u_sb], writes=[ps_S])
                    step("vector", lambda e: e.tensor_copy(out=s_sb[:, :], in_=ps_S[:, 0:512]),
                         reads=[ps_S], writes=[s_sb])

                for cc, dp in ((0, dpos0), (1, dpos1)):
                    dmat(DLT[:, (2 * h + cc) * L:(2 * h + cc + 1) * L].rearrange("p (j f) -> p j f", f=128),
                         dp[:, 0:2048].rearrange("p (j f) -> p j f", f=128),
                         reads=[dp], writes=[DLT])

            if dbg:
                for c_ in range(NCH):
                    dma(dbg_d[128 * c_:128 * c_ + 128, tokb:tokb + L], DLT[:, c_ * L:(c_ + 1) * L],
                        reads=[DLT], eng="gpsimd")
            dma(AR[:, 0:14336].rearrange("p (kc m) -> p kc m", m=1024),
                IN["fg1T"].rearrange("(kc p) m -> p kc m", p=128), writes=[AR], eng="gpsimd")
            for tt in range(NTT):
                col = tt * TT
                barrier()
                for c in range(NCH):
                    conv(VFM, c * PL + PAD + col, fir_s, c * TT, WS["firS"], 3 * c, 3, TT)
                    conv(VFM, c * PL + PAD + col, fir_l, c * TT, WS["firL"], 31 * c, 31, TT)
                for name, srcten, scol, sw in (("s", fir_s, 0, TT), ("l", fir_l, 0, TT), ("d", DLT, col, L)):
                    base = {"s": 0, "l": 2, "d": 4}[name]
                    for cc in range(2):
                        def mfn(e, srcten=srcten, scol=scol, cc=cc, base=base, sw=sw):
                            e.tensor_tensor(out=sq_sb[:, 0:TT], in0=srcten[:, cc * sw + scol:cc * sw + scol + TT],
                                            in1=srcten[:, (cc + 2) * sw + scol:(cc + 2) * sw + scol + TT],
                                            op=ALU.add)
                            e.tensor_tensor(out=sq_sb[:, 512:1024],
                                            in0=srcten[:, (cc + 4) * sw + scol:(cc + 4) * sw + scol + TT],
                                            in1=srcten[:, (cc + 6) * sw + scol:(cc + 6) * sw + scol + TT],
                                            op=ALU.add)
                            e.tensor_tensor(out=sq_sb[:, 0:TT], in0=sq_sb[:, 0:TT], in1=sq_sb[:, 512:1024],
                                            op=ALU.add)
                            return e.tensor_scalar(out=mean_sb[:, (base + cc) * TT:(base + cc + 1) * TT],
                                                   in0=sq_sb[:, 0:TT], scalar1=0.25, scalar2=None, op0=ALU.mult)
                        step("vector", mfn, reads=[srcten], writes=[mean_sb, sq_sb])

                for mc in range(8):
                    pp = ps_mm if mc % 2 == 0 else ps_mm2
                    def mmg1(e, mc=mc, pp=pp, col=col):
                        ins = None
                        for kc in range(14):
                            if kc < 8:
                                rhs = XLN[:, kc * L + col:kc * L + col + TT]
                            else:
                                rhs = mean_sb[:, (kc - 8) * TT:(kc - 7) * TT]
                            ins = e.matmul(pp[:, 0:TT], AR[:, kc * 1024 + 128 * mc:kc * 1024 + 128 * mc + 128],
                                           rhs, start=(kc == 0), stop=(kc == 13))
                        return ins
                    step("tensor", mmg1, reads=[XLN, mean_sb, AR], writes=[pp])
                    step("scalar", lambda e, mc=mc, pp=pp: e.activation(
                        out=ghid_sb[:, mc * TT:(mc + 1) * TT], in_=pp[:, 0:TT], func=ACT.Gelu,
                        bias=WS["fgb1"][:, mc:mc + 1]),
                        reads=[pp, WS["fgb1"]], writes=[ghid_sb])
                def mmg2(e):
                    ins = None
                    for kc in range(8):
                        ins = e.matmul(ps_t[0:4, 0:TT], w_fg2T[:, 4 * kc:4 * kc + 4],
                                       ghid_sb[:, kc * TT:(kc + 1) * TT], start=(kc == 0), stop=(kc == 7))
                    return ins
                step("tensor", mmg2, reads=[ghid_sb, w_fg2T], writes=[ps_t])
                step("scalar", lambda e: e.activation(out=bt_f32[0:4, 0:TT], in_=ps_t[0:4, 0:TT],
                                                      func=ACT.Identity, bias=WS["fgb2"][0:4, 0:1]),
                     reads=[ps_t, WS["fgb2"]], writes=[bt_f32])
                slots = [xt_t[:, 0:512], xt_t[:, 512:1024], ao_tm[:, 0:512], ao_tm[:, 512:1024]]
                smark = [xt_t, xt_t, ao_tm, ao_tm]
                mxs = lnsq[:, 512:1024]
                ssm = rms_sb[:, 0:512]
                for i in range(4):
                    pp = ps_mm if i % 2 == 0 else ps_mm2
                    step("gpsimd", lambda e, i=i: e.dma_start(
                        out=wr4_sb[0:1, 0:TT], in_=bt_f32[i:i + 1, 0:TT]),
                        reads=[bt_f32], writes=[wr4_sb], sem="gs", amt=16)
                    step("tensor", lambda e, pp=pp: e.matmul(pp[:, 0:TT], C["ones1f"][0:1, :],
                                                             wr4_sb[0:1, 0:TT],
                                                             start=True, stop=True),
                         reads=[wr4_sb, C["ones1f"]], writes=[pp])
                    step("vector", lambda e, i=i, pp=pp: e.tensor_scalar(
                        out=slots[i], in0=pp[:, 0:TT], scalar1=itmp_sb[:, 0:1], scalar2=None,
                        op0=ALU.mult),
                        reads=[pp, itmp_sb], writes=[smark[i]])
                def smx(e):
                    e.tensor_tensor(out=mxs, in0=slots[0], in1=slots[1], op=ALU.max)
                    e.tensor_tensor(out=ssm, in0=slots[2], in1=slots[3], op=ALU.max)
                    e.tensor_tensor(out=mxs, in0=mxs, in1=ssm, op=ALU.max)
                    e.tensor_tensor(out=slots[0], in0=slots[0], in1=mxs, op=ALU.subtract)
                    e.tensor_tensor(out=slots[1], in0=slots[1], in1=mxs, op=ALU.subtract)
                    e.tensor_tensor(out=slots[2], in0=slots[2], in1=mxs, op=ALU.subtract)
                    return e.tensor_tensor(out=slots[3], in0=slots[3], in1=mxs, op=ALU.subtract)
                step("vector", smx, reads=[xt_t, ao_tm], writes=[xt_t, ao_tm, lnsq, rms_sb])
                def sexp(e):
                    ins = None
                    for i in range(4):
                        ins = e.activation(out=slots[i], in_=slots[i], func=ACT.Exp)
                    return ins
                step("scalar", sexp, reads=[xt_t, ao_tm], writes=[xt_t, ao_tm])
                def ssum1(e):
                    e.tensor_tensor(out=ssm, in0=slots[0], in1=slots[1], op=ALU.add)
                    e.tensor_tensor(out=ssm, in0=ssm, in1=slots[2], op=ALU.add)
                    e.tensor_tensor(out=ssm, in0=ssm, in1=slots[3], op=ALU.add)
                    return e.reciprocal(out=mxs, in_=ssm)
                step("vector", ssum1, reads=[xt_t, ao_tm], writes=[rms_sb, lnsq])
                step("scalar", lambda e: e.copy(out=ssm, in_=mxs), reads=[lnsq], writes=[rms_sb])
                def snrm(e):
                    ins = None
                    for i in range(4):
                        ins = e.tensor_tensor(out=slots[i], in0=slots[i], in1=ssm, op=ALU.mult)
                    return ins
                step("vector", snrm, reads=[rms_sb, xt_t, ao_tm], writes=[xt_t, ao_tm])
                def sfloor(e):
                    e.tensor_scalar(out=mxs, in0=slots[3], scalar1=0.05, scalar2=None, op0=ALU.mult)
                    e.tensor_tensor(out=slots[0], in0=slots[0], in1=mxs, op=ALU.max)
                    e.tensor_tensor(out=slots[1], in0=slots[1], in1=mxs, op=ALU.max)
                    e.tensor_tensor(out=ssm, in0=slots[0], in1=slots[1], op=ALU.add)
                    e.tensor_tensor(out=ssm, in0=ssm, in1=slots[2], op=ALU.add)
                    e.tensor_tensor(out=ssm, in0=ssm, in1=slots[3], op=ALU.add)
                    return e.reciprocal(out=mxs, in_=ssm)
                step("vector", sfloor, reads=[xt_t, ao_tm], writes=[xt_t, ao_tm, rms_sb, lnsq])
                step("scalar", lambda e: e.copy(out=ssm, in_=mxs), reads=[lnsq], writes=[rms_sb])
                def sfin(e):
                    ins = None
                    for i in range(4):
                        e.tensor_tensor(out=slots[i], in0=slots[i], in1=ssm, op=ALU.mult)
                        ins = e.tensor_copy(out=w4b_sb[:, i * TT:(i + 1) * TT], in_=slots[i])
                    return ins
                step("vector", sfin, reads=[rms_sb, xt_t, ao_tm], writes=[xt_t, ao_tm, w4b_sb])
                for c in range(NCH):
                    def cfn(e, c=c, col=col):
                        e.tensor_tensor(out=sq_sb[:, 0:TT], in0=fir_s[:, c * TT:(c + 1) * TT],
                                        in1=w4b_sb[:, 0:TT], op=ALU.mult)
                        e.tensor_tensor(out=sq_sb[:, 512:1024], in0=fir_l[:, c * TT:(c + 1) * TT],
                                        in1=w4b_sb[:, TT:2 * TT], op=ALU.mult)
                        e.tensor_tensor(out=sq_sb[:, 0:TT], in0=sq_sb[:, 0:TT], in1=sq_sb[:, 512:1024],
                                        op=ALU.add)
                        e.tensor_tensor(out=sq_sb[:, 512:1024], in0=DLT[:, c * L + col:c * L + col + TT],
                                        in1=w4b_sb[:, 2 * TT:3 * TT], op=ALU.mult)
                        e.tensor_tensor(out=sq_sb[:, 0:TT], in0=sq_sb[:, 0:TT], in1=sq_sb[:, 512:1024],
                                        op=ALU.add)
                        e.tensor_tensor(out=sq_sb[:, 512:1024],
                                        in0=VFM[:, c * PL + PAD + col:c * PL + PAD + col + TT],
                                        in1=w4b_sb[:, 3 * TT:4 * TT], op=ALU.mult)
                        return e.tensor_tensor(out=onrm_sb[:, c * TT:(c + 1) * TT], in0=sq_sb[:, 0:TT],
                                               in1=sq_sb[:, 512:1024], op=ALU.add)
                    step("vector", cfn, reads=[fir_s, fir_l, DLT, VFM, w4b_sb], writes=[onrm_sb, sq_sb])
                for hh in range(H):
                    def sqf(e, hh=hh):
                        e.tensor_tensor(out=sq_sb[:, 0:TT], in0=onrm_sb[:, 2 * hh * TT:(2 * hh + 1) * TT],
                                        in1=onrm_sb[:, 2 * hh * TT:(2 * hh + 1) * TT], op=ALU.mult)
                        return e.tensor_tensor(out=sq_sb[:, 512:1024],
                                               in0=onrm_sb[:, (2 * hh + 1) * TT:(2 * hh + 2) * TT],
                                               in1=onrm_sb[:, (2 * hh + 1) * TT:(2 * hh + 2) * TT], op=ALU.mult)
                    step("vector", sqf, reads=[onrm_sb], writes=[sq_sb])
                    def mmrms(e):
                        e.matmul(ps_t[:, 0:TT], C["onesb"][:, :], sq_sb[:, 0:TT], start=True, stop=False)
                        return e.matmul(ps_t[:, 0:TT], C["onesb"][:, :], sq_sb[:, 512:1024],
                                        start=False, stop=True)
                    step("tensor", mmrms, reads=[sq_sb, C["onesb"]], writes=[ps_t])
                    step("vector", lambda e: e.tensor_scalar(out=rms_sb[:, 0:TT], in0=ps_t[:, 0:TT],
                                                             scalar1=1.0 / 256, scalar2=1e-5,
                                                             op0=ALU.mult, op1=ALU.add),
                         reads=[ps_t], writes=[rms_sb])
                    step("vector", lambda e: e.reciprocal(out=rms_sb[:, 0:TT], in_=rms_sb[:, 0:TT]),
                         reads=[rms_sb], writes=[rms_sb])
                    step("scalar", lambda e: e.activation(out=rms_sb[:, 0:TT], in_=rms_sb[:, 0:TT],
                                                          func=ACT.Sqrt),
                         reads=[rms_sb], writes=[rms_sb])
                    for cc in range(2):
                        c = 2 * hh + cc
                        step("vector", lambda e, c=c, cc=cc: e.scalar_tensor_tensor(
                            out=onrm_sb[:, c * TT:(c + 1) * TT], in0=onrm_sb[:, c * TT:(c + 1) * TT],
                            scalar=WS["gnw"][:, cc:cc + 1], in1=rms_sb[:, 0:TT],
                            op0=ALU.mult, op1=ALU.mult),
                            reads=[onrm_sb, WS["gnw"], rms_sb], writes=[onrm_sb])
                if dbg:
                    tc = tokb + col
                    dma(dbg_w[0:4, tc:tc + TT], bt_f32[0:4, 0:TT], reads=[bt_f32], eng="gpsimd", sem="gz")
                    for c_ in range(NCH):
                        dma(dbg_o[128 * c_:128 * c_ + 128, tc:tc + TT],
                            onrm_sb[:, c_ * TT:(c_ + 1) * TT], reads=[onrm_sb], eng="gpsimd", sem="gz")
                    dma(dbg_g[:, tc:tc + TT], ghid_sb[:, 0:TT], reads=[ghid_sb], eng="gpsimd", sem="gz")
                    dma(dbg_l[0:4, tc:tc + TT], bt_f32[0:4, 0:TT], reads=[bt_f32], eng="gpsimd", sem="gz")
                    dma(dbg_fs[:, tc:tc + TT], fir_s[:, 0:TT], reads=[fir_s], eng="gpsimd", sem="gz")
                    dma(dbg_fl[:, tc:tc + TT], fir_l[:, 0:TT], reads=[fir_l], eng="gpsimd", sem="gz")
                    dma(dbg_w4[:, tc:tc + TT], w4b_sb[:, 0:TT], reads=[w4b_sb], eng="gpsimd", sem="gz")
                for mc in range(8):
                    pp = ps_mm if mc % 2 == 0 else ps_mm2
                    wob = wo_st[0]
                    dma(wob[:, :].rearrange("p (kc m) -> p kc m", m=128),
                        IN["woT"].rearrange("(kc p) m -> p kc m", p=128)[:, :, 128 * mc:128 * mc + 128],
                        writes=[wob], eng="gpsimd", sem="gw")
                    def mmop(e, pp=pp, wob=wob):
                        ins = None
                        for kc in range(NCH):
                            ins = e.matmul(pp[:, 0:TT], wob[:, kc * 128:kc * 128 + 128],
                                           onrm_sb[:, kc * TT:(kc + 1) * TT],
                                           start=(kc == 0), stop=(kc == NCH - 1))
                        return ins
                    step("tensor", mmop, reads=[onrm_sb, wob], writes=[pp])
                    step("scalar", lambda e, mc=mc, pp=pp: e.copy(
                        out=aofm_sb[:, :].rearrange("p (b m) -> p b m", m=1024)[:, :, 128 * mc:128 * mc + 128],
                        in_=pp[:, 0:TT].rearrange("p (b f) -> p b f", f=128)),
                        reads=[pp], writes=[aofm_sb])
                for b in range(4):
                    g128 = sq * NT128 + 4 * tt + b
                    r0 = tokb + col + 128 * b
                    dmat(ao_bf[:, :].rearrange("p (c f) -> p c f", f=128),
                         aofm_sb[:, 1024 * b:1024 * b + 1024].rearrange("p (c f) -> p c f", f=128),
                         reads=[aofm_sb], writes=[ao_bf])
                    dma(xt_t[:, :], x_in[r0:r0 + 128, :], writes=[xt_t])
                    ln_stats_apply(ao_bf, ao_tm, xt_t, x2_t)
                    dma(x2_scr[r0:r0 + 128, :], x2_t[:, :], reads=[x2_t])
                    step("vector", lambda e: e.reduce_sum(out=stat_sb[:, 6:7], in_=x2_t[:, :], axis=AXX),
                         reads=[x2_t], writes=[stat_sb])
                    step("scalar", lambda e, g128=g128: e.mul(out=m3_sb[:, g128:g128 + 1],
                                                              in_=stat_sb[:, 6:7], mul=-1.0 / 1024),
                         reads=[stat_sb], writes=[m3_sb])
                    step("vector", lambda e: e.tensor_tensor(out=ao_tm[:, :], in0=x2_t[:, :], in1=x2_t[:, :],
                                                             op=ALU.mult),
                         reads=[x2_t], writes=[ao_tm])
                    step("vector", lambda e: e.reduce_sum(out=stat_sb[:, 7:8], in_=ao_tm[:, :], axis=AXX),
                         reads=[ao_tm], writes=[stat_sb])
                    step("vector", lambda e, g128=g128: e.tensor_tensor(
                        out=stat_sb[:, 8:9], in0=m3_sb[:, g128:g128 + 1], in1=m3_sb[:, g128:g128 + 1],
                        op=ALU.mult),
                        reads=[m3_sb], writes=[stat_sb])
                    step("scalar", lambda e: e.mul(out=stat_sb[:, 9:10], in_=stat_sb[:, 7:8], mul=1.0 / 1024),
                         reads=[stat_sb], writes=[stat_sb])
                    step("vector", lambda e: e.tensor_tensor(out=stat_sb[:, 10:11], in0=stat_sb[:, 9:10],
                                                             in1=stat_sb[:, 8:9], op=ALU.subtract),
                         reads=[stat_sb], writes=[stat_sb])
                    step("scalar", lambda e: e.activation(out=stat_sb[:, 12:13], in_=stat_sb[:, 10:11],
                                                          func=ACT.Sqrt, bias=C["eps5"][:, :]),
                         reads=[stat_sb, C["eps5"]], writes=[stat_sb])
                    step("vector", lambda e, g128=g128: e.reciprocal(out=r3_sb[:, g128:g128 + 1],
                                                                     in_=stat_sb[:, 12:13]),
                         reads=[stat_sb], writes=[r3_sb])
                    step("vector", lambda e: e.reduce_max(out=stat_sb[:, 13:14], in_=x2_t[:, :], axis=AXX),
                         reads=[x2_t], writes=[stat_sb])
                    step("scalar", lambda e: e.copy(out=stat_sb[:, 14:15], in_=stat_sb[:, 13:14]),
                         reads=[stat_sb], writes=[stat_sb])
                    def lmaxfn(e, g128=g128):
                        e.tensor_tensor(out=stat_sb[:, 15:16], in0=stat_sb[:, 14:15],
                                        in1=m3_sb[:, g128:g128 + 1], op=ALU.add)
                        e.tensor_tensor(out=stat_sb[:, 15:16], in0=stat_sb[:, 15:16],
                                        in1=r3_sb[:, g128:g128 + 1], op=ALU.mult)
                        return e.tensor_tensor(out=gmax_sb[:, 0:1], in0=gmax_sb[:, 0:1],
                                               in1=stat_sb[:, 15:16], op=ALU.max)
                    step("vector", lmaxfn, reads=[stat_sb, m3_sb, r3_sb, gmax_sb],
                         writes=[stat_sb, gmax_sb])

        barrier()
        def cross_max_allreduce(srccol, dstcol):
            step("tensor", lambda e: e.transpose(ps_t[0:1, 0:128], gmax_sb[:, srccol:srccol + 1],
                                                 C["idf"][:, :]),
                 reads=[gmax_sb, C["idf"]], writes=[ps_t])
            step("vector", lambda e: e.tensor_copy(out=rms_sb[0:1, 0:128], in_=ps_t[0:1, 0:128]),
                 reads=[ps_t], writes=[rms_sb])
            step("vector", lambda e: e.reduce_max(out=ccst_sb[0:1, 0:1], in_=rms_sb[0:1, 0:128], axis=AXX),
                 reads=[rms_sb], writes=[ccst_sb])
            dma(cc_in[:, :], ccst_sb[:, :], reads=[ccst_sb], eng="gpsimd")
            ev = step("gpsimd", lambda e: e.collective_compute(
                "AllReduce", ALU.max, replica_groups=[[0, 1, 2, 3, 4, 5, 6, 7]],
                ins=[cc_in.opt()], outs=[cc_out.opt()]), sem="g", amt=1,
                extra=[("gd", cnt["gd"])])
            dma(ccst_sb[0:1, 0:1], cc_out[0:1, 0:1], writes=[ccst_sb], eng="gpsimd", extra=[ev])
            step("tensor", lambda e: e.matmul(ps_t[0:128, 0:1], C["ones1f"][0:1, :], ccst_sb[0:1, 0:1],
                                              start=True, stop=True),
                 reads=[ccst_sb, C["ones1f"]], writes=[ps_t])
            step("scalar", lambda e, dstcol=dstcol: e.copy(out=sin_sb[:, dstcol:dstcol + 1],
                                                           in_=ps_t[0:128, 0:1]),
                 reads=[ps_t], writes=[sin_sb])

        for fc in range(NFC):
            dma(lnsq[:, :], IN["w1"][128 * fc:128 * fc + 128, :], writes=[lnsq])
            step("vector", lambda e, fc=fc: e.tensor_reduce(out=sw1_sb[:, fc:fc + 1], in_=lnsq[:, :],
                                                            axis=AXX, op=ALU.max, apply_absolute_value=True),
                 reads=[lnsq], writes=[sw1_sb])
            step("scalar", lambda e, fc=fc: e.mul(out=tiny2_sb[:, fc:fc + 1], in_=sw1_sb[:, fc:fc + 1],
                                                  mul=1.0 / 127),
                 reads=[sw1_sb], writes=[tiny2_sb])
            step("vector", lambda e, fc=fc: e.reciprocal(out=tiny_sb[:, 0:1], in_=tiny2_sb[:, fc:fc + 1]),
                 reads=[tiny2_sb], writes=[tiny_sb])
            step("scalar", lambda e: e.activation(out=wstg2[:, :], in_=lnsq[:, :], func=ACT.Identity,
                                                  scale=tiny_sb[:, 0:1], bias=MG[:, :]),
                 reads=[lnsq, tiny_sb, MG], writes=[wstg2])
            step("vector", lambda e: e.tensor_scalar(out=qw_tile[:, 0:1024], in0=wstg2[:, :],
                                                     scalar1=MAGIC, scalar2=None, op0=ALU.subtract),
                 reads=[wstg2], writes=[qw_tile])
            dmat(W1A[:, :].rearrange("p (dc m) -> p dc m", m=4096)[:, :, 128 * fc:128 * fc + 128],
                 qw_tile[:, 0:1024].rearrange("p (dc f) -> p dc f", f=128)[:, 0:4, :],
                 reads=[qw_tile], writes=[W1A])
            dmat(DLT[:, :].rearrange("p (dc m) -> p dc m", m=4096)[:, :, 128 * fc:128 * fc + 128],
                 qw_tile[:, 0:1024].rearrange("p (dc f) -> p dc f", f=128)[:, 4:8, :],
                 reads=[qw_tile], writes=[DLT])

        cross_max_allreduce(0, 0)
        step("vector", lambda e: e.tensor_scalar(out=sin_sb[:, 0:1], in0=sin_sb[:, 0:1],
                                                 scalar1=1e-8, scalar2=1.0 / 255,
                                                 op0=ALU.max, op1=ALU.mult),
             reads=[sin_sb], writes=[sin_sb])
        step("scalar", lambda e: e.copy(out=sin_sb[:, 0:1], in_=sin_sb[:, 0:1]),
             reads=[sin_sb], writes=[sin_sb])
        step("vector", lambda e: e.reciprocal(out=sin_sb[:, 1:2], in_=sin_sb[:, 0:1]),
             reads=[sin_sb], writes=[sin_sb])
        step("vector", lambda e: e.tensor_scalar(out=a1_sb[:, :], in0=tiny2_sb[:, :],
                                                 scalar1=sin_sb[:, 0:1], scalar2=None, op0=ALU.mult),
             reads=[tiny2_sb, sin_sb], writes=[a1_sb])
        step("vector", lambda e: e.reciprocal(out=tiny_sb[:, 6:7] if False else bq1_sb[:, :], in_=a1_sb[:, :]),
             reads=[a1_sb], writes=[bq1_sb])
        step("scalar", lambda e: e.copy(out=bq1_sb[:, :], in_=bq1_sb[:, :]),
             reads=[bq1_sb], writes=[bq1_sb])
        step("vector", lambda e: e.tensor_tensor(out=bq1_sb[:, :], in0=WS["b1"][:, :], in1=bq1_sb[:, :],
                                                 op=ALU.mult),
             reads=[WS["b1"], bq1_sb], writes=[bq1_sb])
        step("scalar", lambda e: e.activation(out=bq1_sb[:, :], in_=bq1_sb[:, :], func=ACT.Identity,
                                              bias=MG[:, :]),
             reads=[bq1_sb, MG], writes=[bq1_sb])
        step("vector", lambda e: e.tensor_scalar(out=bq1_sb[:, :], in0=bq1_sb[:, :], scalar1=MAGIC,
                                                 scalar2=None, op0=ALU.subtract),
             reads=[bq1_sb], writes=[bq1_sb])
        step("vector", lambda e: e.tensor_tensor(out=bq1_sb[:, :], in0=bq1_sb[:, :], in1=a1_sb[:, :],
                                                 op=ALU.mult),
             reads=[bq1_sb, a1_sb], writes=[bq1_sb])

        def w1_lhsT(dc, fc):
            if dc < 4:
                return W1A[:, dc * 4096 + 128 * fc:dc * 4096 + 128 * fc + 128]
            return DLT[:, (dc - 4) * 4096 + 128 * fc:(dc - 4) * 4096 + 128 * fc + 128]

        for gt in range(TOK // TT):
            col0 = gt * TT
            for b in range(4):
                r0 = col0 + 128 * b
                g128 = r0 // 128
                dma(xt_t[:, :], x2_scr[r0:r0 + 128, :], writes=[xt_t])
                step("vector", lambda e, g128=g128: e.tensor_scalar(
                    out=x2_t[:, :], in0=xt_t[:, :], scalar1=m3_sb[:, g128:g128 + 1],
                    scalar2=r3_sb[:, g128:g128 + 1], op0=ALU.add, op1=ALU.mult),
                    reads=[xt_t, m3_sb, r3_sb], writes=[x2_t])
                step("scalar", lambda e: e.activation(out=x2_t[:, :], in_=x2_t[:, :], func=ACT.Identity,
                                                      scale=sin_sb[:, 1:2], bias=MG[:, :]),
                     reads=[x2_t, sin_sb, MG], writes=[x2_t])
                step("vector", lambda e: e.tensor_scalar(out=xq_tm[:, :], in0=x2_t[:, :],
                                                         scalar1=MAGIC, scalar2=0.0,
                                                         op0=ALU.subtract, op1=ALU.max),
                     reads=[x2_t], writes=[xq_tm])
                dmat(xq_fm.rearrange("p (c tk) -> p c tk", tk=TT)[:, :, 128 * b:128 * b + 128],
                     xq_tm[:, :].rearrange("p (c f) -> p c f", f=128),
                     reads=[xq_tm], writes=[xq_fm])
            for fc in range(NFC):
                pp = ps_mm if fc % 2 == 0 else ps_mm2
                hb = ht[fc % 2]
                def mmf1(e, fc=fc, pp=pp):
                    ins = None
                    for dc in range(8):
                        ins = e.matmul(pp[:, 0:TT], w1_lhsT(dc, fc), xq_fm[:, dc * TT:(dc + 1) * TT],
                                       start=(dc == 0), stop=(dc == 7))
                    return ins
                step("tensor", mmf1, reads=[xq_fm, W1A, DLT], writes=[pp])
                step("scalar", lambda e, fc=fc, pp=pp, hb=hb: e.activation(
                    out=hb[:, :], in_=pp[:, 0:TT], func=ACT.Relu,
                    scale=a1_sb[:, fc:fc + 1], bias=bq1_sb[:, fc:fc + 1]),
                    reads=[pp, a1_sb, bq1_sb], writes=[hb])
                step("vector", lambda e, hb=hb: e.reduce_max(out=stat_sb[:, 13:14], in_=hb[:, :], axis=AXX),
                     reads=[hb], writes=[stat_sb])
                step("scalar", lambda e: e.copy(out=stat_sb[:, 15:16], in_=stat_sb[:, 13:14]),
                     reads=[stat_sb], writes=[stat_sb])
                step("vector", lambda e: e.tensor_tensor(out=gmax_sb[:, 1:2], in0=gmax_sb[:, 1:2],
                                                         in1=stat_sb[:, 15:16], op=ALU.max),
                     reads=[stat_sb, gmax_sb], writes=[gmax_sb])
                dma(h_scr[fc, :, col0:col0 + TT], hb[:, :], reads=[hb])

        for dc in range(8):
            for qq in range(4):
                dma(lnsq[:, :], IN["w2"][128 * dc:128 * dc + 128, 1024 * qq:1024 * qq + 1024], writes=[lnsq])
                step("vector", lambda e: e.tensor_reduce(out=tiny_sb[:, 4:5], in_=lnsq[:, :],
                                                          axis=AXX, op=ALU.max, apply_absolute_value=True),
                    reads=[lnsq], writes=[tiny_sb])
                step("scalar", lambda e: e.copy(out=tiny_sb[:, 5:6], in_=tiny_sb[:, 4:5]),
                     reads=[tiny_sb], writes=[tiny_sb])
                if qq == 0:
                    step("vector", lambda e, dc=dc: e.tensor_copy(out=sw2_sb[:, dc:dc + 1], in_=tiny_sb[:, 5:6]),
                         reads=[tiny_sb], writes=[sw2_sb])
                else:
                    step("vector", lambda e, dc=dc: e.tensor_tensor(out=sw2_sb[:, dc:dc + 1],
                                                                    in0=sw2_sb[:, dc:dc + 1],
                                                                    in1=tiny_sb[:, 5:6], op=ALU.max),
                         reads=[tiny_sb, sw2_sb], writes=[sw2_sb])
            step("scalar", lambda e, dc=dc: e.mul(out=a2_sb[:, dc:dc + 1], in_=sw2_sb[:, dc:dc + 1],
                                                  mul=1.0 / 127),
                 reads=[sw2_sb], writes=[a2_sb])
            step("vector", lambda e, dc=dc: e.reciprocal(out=tiny_sb[:, 0:1], in_=a2_sb[:, dc:dc + 1]),
                 reads=[a2_sb], writes=[tiny_sb])
            for qq in range(4):
                dma(lnsq[:, :], IN["w2"][128 * dc:128 * dc + 128, 1024 * qq:1024 * qq + 1024], writes=[lnsq])
                step("scalar", lambda e: e.activation(out=wstg2[:, :], in_=lnsq[:, :], func=ACT.Identity,
                                                      scale=tiny_sb[:, 0:1], bias=MG[:, :]),
                     reads=[lnsq, tiny_sb, MG], writes=[wstg2])
                step("vector", lambda e: e.tensor_scalar(out=qw_tile[:, 0:1024], in0=wstg2[:, :],
                                                         scalar1=MAGIC, scalar2=None, op0=ALU.subtract),
                     reads=[wstg2], writes=[qw_tile])
                fclo = 8 * qq
                ins_ap = qw_tile[:, 0:1024].rearrange("p (fc f) -> p fc f", f=128)
                if fclo + 8 <= 16:
                    dmat(VFM[:, 0:16384].rearrange("p (fc m) -> p fc m", m=1024)[:, fclo:fclo + 8,
                         128 * dc:128 * dc + 128], ins_ap, reads=[qw_tile], writes=[VFM])
                elif fclo >= 16 and fclo + 8 <= 30:
                    dmat(AR[:, :].rearrange("p (fc m) -> p fc m", m=1024)[:, fclo - 16:fclo - 8,
                         128 * dc:128 * dc + 128], ins_ap, reads=[qw_tile], writes=[AR])
                else:
                    dmat(AR[:, :].rearrange("p (fc m) -> p fc m", m=1024)[:, 8:14,
                         128 * dc:128 * dc + 128], ins_ap[:, 0:6, :], reads=[qw_tile], writes=[AR])
                    dmat(w2qC.rearrange("p (fc m) -> p fc m", m=1024)[:, 0:2,
                         128 * dc:128 * dc + 128], ins_ap[:, 6:8, :], reads=[qw_tile], writes=[w2qC])

        cross_max_allreduce(1, 2)
        step("vector", lambda e: e.tensor_scalar(out=sin_sb[:, 2:3], in0=sin_sb[:, 2:3],
                                                 scalar1=1e-8, scalar2=1.0 / 255,
                                                 op0=ALU.max, op1=ALU.mult),
             reads=[sin_sb], writes=[sin_sb])
        step("scalar", lambda e: e.copy(out=sin_sb[:, 2:3], in_=sin_sb[:, 2:3]),
             reads=[sin_sb], writes=[sin_sb])
        step("vector", lambda e: e.reciprocal(out=sin_sb[:, 3:4], in_=sin_sb[:, 2:3]),
             reads=[sin_sb], writes=[sin_sb])
        step("vector", lambda e: e.tensor_scalar(out=a2_sb[:, :], in0=a2_sb[:, :],
                                                 scalar1=sin_sb[:, 2:3], scalar2=None, op0=ALU.mult),
             reads=[a2_sb, sin_sb], writes=[a2_sb])
        step("vector", lambda e: e.reciprocal(out=tiny_sb[:, 6:7] if False else bq2_sb[:, :], in_=a2_sb[:, :]),
             reads=[a2_sb], writes=[bq2_sb])
        step("scalar", lambda e: e.copy(out=bq2_sb[:, :], in_=bq2_sb[:, :]),
             reads=[bq2_sb], writes=[bq2_sb])
        step("vector", lambda e: e.tensor_tensor(out=bq2_sb[:, :], in0=WS["b2"][:, :], in1=bq2_sb[:, :],
                                                 op=ALU.mult),
             reads=[WS["b2"], bq2_sb], writes=[bq2_sb])
        step("scalar", lambda e: e.activation(out=bq2_sb[:, :], in_=bq2_sb[:, :], func=ACT.Identity,
                                              bias=MG[:, :]),
             reads=[bq2_sb, MG], writes=[bq2_sb])
        step("vector", lambda e: e.tensor_scalar(out=bq2_sb[:, :], in0=bq2_sb[:, :], scalar1=MAGIC,
                                                 scalar2=None, op0=ALU.subtract),
             reads=[bq2_sb], writes=[bq2_sb])
        step("vector", lambda e: e.tensor_tensor(out=bq2_sb[:, :], in0=bq2_sb[:, :], in1=a2_sb[:, :],
                                                 op=ALU.mult),
             reads=[bq2_sb, a2_sb], writes=[bq2_sb])

        def w2_lhsT(fc, dc):
            if fc < 16:
                return VFM[:, fc * 1024 + 128 * dc:fc * 1024 + 128 * dc + 128]
            if fc < 30:
                return AR[:, (fc - 16) * 1024 + 128 * dc:(fc - 16) * 1024 + 128 * dc + 128]
            return w2qC[:, (fc - 30) * 1024 + 128 * dc:(fc - 30) * 1024 + 128 * dc + 128]

        for gt in range(TOK // TT):
            col0 = gt * TT
            for fc in range(NFC):
                hb = ht[fc % 2]
                dma(hb[:, :], h_scr[fc, :, col0:col0 + TT], writes=[hb])
                step("scalar", lambda e, hb=hb: e.activation(out=hb[:, :], in_=hb[:, :], func=ACT.Identity,
                                                             scale=sin_sb[:, 3:4], bias=MG[:, :]),
                     reads=[hb, sin_sb, MG], writes=[hb])
                step("vector", lambda e, fc=fc, hb=hb: e.tensor_scalar(
                    out=XLN[:, fc * TT:(fc + 1) * TT], in0=hb[:, :],
                    scalar1=MAGIC, scalar2=None, op0=ALU.subtract),
                    reads=[hb], writes=[XLN])
            for dc in range(8):
                pp = ps_mm if dc % 2 == 0 else ps_mm2
                def mmf2(e, dc=dc, pp=pp):
                    ins = None
                    for fc in range(NFC):
                        ins = e.matmul(pp[:, 0:TT], w2_lhsT(fc, dc), XLN[:, fc * TT:(fc + 1) * TT],
                                       start=(fc == 0), stop=(fc == NFC - 1))
                    return ins
                step("tensor", mmf2, reads=[XLN, VFM, AR, w2qC], writes=[pp])
                step("scalar", lambda e, dc=dc, pp=pp: e.activation(
                    out=aofm_sb[:, :].rearrange("p (b m) -> p b m", m=1024)[:, :, 128 * dc:128 * dc + 128],
                    in_=pp[:, 0:TT].rearrange("p (b f) -> p b f", f=128), func=ACT.Identity,
                    scale=a2_sb[:, dc:dc + 1], bias=bq2_sb[:, dc:dc + 1]),
                    reads=[pp, a2_sb, bq2_sb], writes=[aofm_sb])
            for b in range(4):
                r0 = col0 + 128 * b
                dmat(ao_bf[:, :].rearrange("p (c f) -> p c f", f=128),
                     aofm_sb[:, 1024 * b:1024 * b + 1024].rearrange("p (c f) -> p c f", f=128),
                     reads=[aofm_sb], writes=[ao_bf])
                dma(xt_t[:, :], x2_scr[r0:r0 + 128, :], writes=[xt_t])
                ln_stats_apply(ao_bf, ao_tm, xt_t, x2_t)
                step("sync", lambda e, r0=r0: e.dma_start(out=y_out[r0:r0 + 128, :], in_=x2_t[:, :]),
                     reads=[x2_t], sem="st", amt=16)

        steps["sync"].append(([("st", cnt["st"])], None, None))
        import sys
        print("SEM COUNTS:", cnt, file=sys.stderr)
        print("STEPS:", {k: len(v) for k, v in steps.items()}, file=sys.stderr)

        def replay(eng, engine_obj):
            seen = {s: 0 for s in semnames}
            for waits, fn, inc in steps[eng]:
                for smn, val in waits:
                    if val > seen[smn]:
                        engine_obj.wait_ge(sems[smn], val)
                        seen[smn] = val
                if fn is not None:
                    ins = fn(engine_obj)
                    if inc is not None:
                        ins.then_inc(sems[inc[0]], inc[1])

        @block.sync
        def _(sync):
            replay("sync", sync)

        @block.tensor
        def _(tensor):
            replay("tensor", tensor)

        @block.vector
        def _(vector):
            replay("vector", vector)

        @block.scalar
        def _(scalar):
            replay("scalar", scalar)

        @block.gpsimd
        def _(gpsimd):
            replay("gpsimd", gpsimd)

    return nc


_NC_CACHE = [None]


def kernel(**kw):
    g = lambda n: np.ascontiguousarray(np.asarray(kw[n], dtype=np.float32))
    x = g("L0_attn_hidden_states")
    lay = lambda a: np.ascontiguousarray(a)
    cwtile = lambda w: lay(w.reshape(8, 128, w.shape[-1]).transpose(1, 0, 2).reshape(128, -1))
    weights = {
        "wqT": lay(g("L0_attn_q_proj_w").T), "wkT": lay(g("L0_attn_k_proj_w").T),
        "wvT": lay(g("L0_attn_v_proj_w").T), "wbT": lay(g("L0_attn_b_proj_w").T),
        "qcw": cwtile(g("L0_attn_q_conv_w")), "kcw": cwtile(g("L0_attn_k_conv_w")),
        "vcw": cwtile(g("L0_attn_v_conv_w")),
        "firS": cwtile(g("L0_attn_fir_short_w").reshape(1024, 3)),
        "firL": cwtile(g("L0_attn_fir_long_w").reshape(1024, 31)),
        "fg1T": lay(g("L0_attn_fg_w1").T), "fgb1": lay(g("L0_attn_fg_b1").reshape(8, 128).T),
        "fg2T": lay(g("L0_attn_fg_w2").T), "fgb2": lay(g("L0_attn_fg_b2").reshape(4, 1)),
        "ltemp": lay(g("L0_attn_logit_temp").reshape(1, 1)),
        "gnw": lay(g("L0_attn_o_norm_w").reshape(2, 128).T),
        "woT": lay(g("L0_attn_o_proj_w").T),
        "w1": g("L0_ffn_w1"), "b1": lay(g("L0_ffn_b1").reshape(32, 128).T),
        "w2": g("L0_ffn_w2"), "b2": lay(g("L0_ffn_b2").reshape(8, 128).T),
    }
    if _NC_CACHE[0] is None:
        _NC_CACHE[0] = build()
    nc = _NC_CACHE[0]
    in_maps = []
    for c in range(8):
        m = {"x": lay(x[4 * c:4 * c + 4].reshape(TOK, D))}
        m.update(weights)
        in_maps.append(m)
    res = run_bass_kernel_spmd(nc, in_maps, core_ids=list(range(8)))
    out = np.concatenate([np.asarray(res.results[c]["y"]).reshape(4, L, D) for c in range(8)], axis=0)
    return out.astype(np.float32)


# version 10
# speedup vs baseline: 1.0428x; 1.0428x over previous
import numpy as np
import ml_dtypes
import concourse.bass as bass
import concourse.mybir as mybir
from concourse.bass_utils import run_bass_kernel_spmd
from contextlib import ExitStack

F32 = mybir.dt.float32
BF16 = mybir.dt.bfloat16
ALU = mybir.AluOpType
ACT = mybir.ActivationFunctionType
AXX = mybir.AxisListType.X

NSEQ, L, D, H = 4, 2048, 1024, 4
NCH = 8
CS = 128
NCK = L // CS
F = 4096
NFC = 32
TT = 512
NTT = L // TT
NT128 = L // 128
TOK = NSEQ * L
MAGIC = 12582912.0
PAD = 32
PL = PAD + L

ENGINES = ("sync", "tensor", "vector", "scalar", "gpsimd")
ESEM = {"sync": "d", "tensor": "te", "vector": "v", "scalar": "s", "gpsimd": "g"}


class Buf:
    __slots__ = ("wr", "rds")

    def __init__(self):
        self.wr = None
        self.rds = []


def build(dbg=False):
    nc = bass.Bass("TRN2", target_bir_lowering=False, debug=False)

    x_in = nc.dram_tensor("x", [TOK, D], F32, kind="ExternalInput").ap()
    shapes = {
        "wqT": [D, D], "wkT": [D, D], "wvT": [D, D], "wbT": [D, H],
        "qcw": [128, 32], "kcw": [128, 32], "vcw": [128, 32],
        "firS": [128, 24], "firL": [128, 248],
        "fg1T": [1792, D], "fgb1": [128, 8], "fg2T": [D, H], "fgb2": [H, 1],
        "ltemp": [1, 1], "gnw": [128, 2], "woT": [D, D],
        "w1": [F, D], "b1": [128, 32], "w2": [D, F], "b2": [128, 8],
    }
    IN = {n: nc.dram_tensor(n, sh, F32, kind="ExternalInput").ap() for n, sh in shapes.items()}
    y_out = nc.dram_tensor("y", [TOK, D], F32, kind="ExternalOutput").ap()

    x2_scr = nc.dram_tensor("x2_scr", [TOK, D], F32,
                            kind="ExternalOutput" if dbg else "Internal").ap()
    h_scr = nc.dram_tensor("h_scr", [NFC, 128, TOK], F32,
                           kind="ExternalOutput" if dbg else "Internal").ap()
    dbg_d = nc.dram_tensor("dbg_d", [NCH * 128, TOK], F32,
                           kind="ExternalOutput").ap() if dbg else None
    dbg_w = nc.dram_tensor("dbg_w", [4, TOK], F32, kind="ExternalOutput").ap() if dbg else None
    dbg_o = nc.dram_tensor("dbg_o", [NCH * 128, TOK], F32, kind="ExternalOutput").ap() if dbg else None
    dbg_g = nc.dram_tensor("dbg_g", [128, TOK], F32, kind="ExternalOutput").ap() if dbg else None
    dbg_l = nc.dram_tensor("dbg_l", [4, TOK], F32, kind="ExternalOutput").ap() if dbg else None
    dbg_fs = nc.dram_tensor("dbg_fs", [128, TOK], F32, kind="ExternalOutput").ap() if dbg else None
    dbg_fl = nc.dram_tensor("dbg_fl", [128, TOK], F32, kind="ExternalOutput").ap() if dbg else None
    dbg_w4 = nc.dram_tensor("dbg_w4", [128, TOK], F32, kind="ExternalOutput").ap() if dbg else None
    cc_in = nc.dram_tensor("cc_in", [128, 128], F32, kind="Internal").ap()
    cc_out = nc.dram_tensor("cc_out", [128, 128], F32, kind="Internal").ap()

    consts = {
        "idb": np.eye(128, dtype=ml_dtypes.bfloat16),
        "idf": np.eye(128, dtype=np.float32),
        "mlo": np.tril(np.ones((128, 128), np.float32), -1).astype(ml_dtypes.bfloat16),
        "mup": np.triu(np.ones((128, 128), np.float32), 1).astype(ml_dtypes.bfloat16),
        "mupi": np.triu(np.ones((128, 128), np.float32), 0).astype(ml_dtypes.bfloat16),
        "onesb": np.ones((128, 128), ml_dtypes.bfloat16),
        "ones1f": np.ones((1, 128), np.float32),
        "ones1b": np.ones((1, 128), ml_dtypes.bfloat16),
        "magicc": np.full((128, 1), MAGIC, np.float32),
        "i1024": np.full((128, 1), 1.0 / 1024, np.float32),
        "eps5": np.full((128, 1), 1e-5, np.float32),
    }
    CIN = {n: nc.inline_tensor(v, name="c_" + n).ap() for n, v in consts.items()}

    semnames = ["d", "d1", "d2", "d3", "d4", "d5", "d6", "d7", "ds", "dt", "dt1", "dt2", "dt3", "te", "v", "s", "g", "gd", "gd1", "gs", "gs1", "gw", "gw1", "gw2", "gw3", "gz", "st"]
    steps = {e: [] for e in ENGINES}
    cnt = {s: 0 for s in semnames}
    BUFS = {}

    def buf(t):
        k = id(t)
        if k not in BUFS:
            BUFS[k] = Buf()
        return BUFS[k]

    MARK = {}

    def mark_of(t):
        return MARK.get(id(t), t)

    def step(eng, fn, reads=(), writes=(), extra=(), sem=None, amt=1):
        rb = [buf(mark_of(t)) for t in reads]
        wb = [buf(mark_of(t)) for t in writes]
        waits = []
        for b in rb:
            if b.wr is not None:
                waits.append(b.wr)
        for b in wb:
            waits += b.rds
            if b.wr is not None:
                waits.append(b.wr)
        waits += [w for w in extra if w is not None]
        if sem is None:
            sem = ESEM[eng]
        cnt[sem] += amt
        ev = (sem, cnt[sem])
        steps[eng].append((waits, fn, (sem, amt)))
        for b in rb:
            b.rds.append(ev)
        for b in wb:
            b.wr = ev
            b.rds = []
        return ev

    _last_dma = {}
    _POOL = {"d": ["d", "d1", "d2", "d3", "d4", "d5", "d6", "d7"], "gs": ["gs", "gs1"], "gw": ["gw", "gw1", "gw2", "gw3"]}
    _rr = {k: 0 for k in _POOL}

    def dma(dst, src, reads=(), writes=(), eng="sync", extra=(), sem=None):
        if sem is None:
            sem = "gd" if eng == "gpsimd" else "d"
        if sem in _POOL:
            p = _POOL[sem]
            sem = p[_rr[sem] % len(p)]
            _rr[p[0]] += 1
        ex = list(extra)
        if sem in _last_dma:
            ex.append(_last_dma[sem])
        ev = step(eng, lambda e: e.dma_start(out=dst, in_=src),
                  reads=reads, writes=writes, extra=ex, sem=sem, amt=16)
        _last_dma[sem] = ev
        return ev

    _DTP = ["dt", "dt1", "dt2", "dt3"]
    _last_dt = {}
    _dtr = [0]

    def dmat(dst, src, reads=(), writes=(), extra=()):
        ex = list(extra)
        sem = _DTP[_dtr[0] % len(_DTP)]
        _dtr[0] += 1
        if sem in _last_dt:
            ex.append(_last_dt[sem])
        ev = step("sync", lambda e: e.dma_start_transpose(out=dst, in_=src),
                  reads=reads, writes=writes, extra=ex, sem=sem, amt=16)
        _last_dt[sem] = ev
        return ev

    with ExitStack() as ctx:
        def sb(name, shape, dt):
            return ctx.enter_context(nc.sbuf_tensor(name, shape, dt))

        def region(arena, c0, c1, name=None):
            ap = arena[:, c0:c1]
            MARK[id(ap)] = arena
            return ap

        C = {n: sb("k_" + n, list(consts[n].shape),
                   BF16 if consts[n].dtype == ml_dtypes.bfloat16 else F32) for n in consts}
        WS = {}
        for n, sh in [("qcw", [128, 32]), ("kcw", [128, 32]), ("vcw", [128, 32]),
                      ("firS", [128, 24]), ("firL", [128, 248]), ("fgb1", [128, 8]),
                      ("fgb2", [4, 1]), ("gnw", [128, 2]), ("b1", [128, 32]), ("b2", [128, 8])]:
            WS[n] = sb("w_" + n, sh, F32)
        w_wbT = sb("w_wbT", [128, 32], BF16)
        w_fg2T = sb("w_fg2T", [128, 32], BF16)

        XLN = sb("XLN", [128, 16384], BF16)
        DLT = sb("DLT", [128, 16384], BF16)
        VFM = sb("VFM", [128, NCH * PL], BF16)
        W1A = sb("W1A", [128, 8192], BF16)
        W1B = sb("W1B", [128, 8192], BF16)
        AR = sb("AR", [128, 14336], BF16)
        KTA = sb("KTA", [128, 4160], BF16)

        kpos = region(W1A, 0, 4096)
        vpos = region(W1A, 4096, 8192)
        dpos0 = region(W1B, 0, 2048)
        dpos1 = region(W1B, 2048, 4096)
        rt_s = region(W1B, 4096, 6144)
        at_s = region(W1B, 6144, 8192)
        fir_s = region(W1A, 0, 4096)
        fir_l = region(W1A, 4096, 8192)
        ghid_sb = region(W1B, 0, 4096)
        onrm_sb = region(W1B, 4096, 8192)

        q_fm = region(AR, 0, 4160)
        k_fm = region(AR, 4160, 8320)
        wqk = region(AR, 8320, 12416)

        pre_cs = [region(KTA, 0, 2080), region(KTA, 2080, 4160)]
        xq_fm = region(KTA, 0, 4096)
        w2qC = region(KTA, 0, 4096)

        qw_tile = region(XLN, 0, 4096)

        def regt(arena, p0, p1, c0, c1):
            ap = arena[p0:p1, c0:c1]
            MARK[id(ap)] = arena
            return ap

        s_sb = sb("s_sb", [128, 512], BF16)
        z_sb = sb("z_sb", [128, 256], BF16)
        u_sb = sb("u_sb", [128, 256], BF16)
        xch = [sb("xch0", [128, 128], F32), sb("xch1", [128, 128], F32)]
        bch = [sb("bch0", [128, 128], F32), sb("bch1", [128, 128], F32)]
        rch = [sb("rch0", [128, 128], F32), sb("rch1", [128, 128], F32)]
        sq_sb = sb("sq_sb", [128, 1024], BF16)
        bcb_sb = region(AR, 12416, 12928)
        bpos_sb = sb("bpos_sb", [128, 64], F32)
        itmp_sb = sb("itmp_sb", [128, 1], F32)
        sc1_sb = sb("sc1_sb", [1, 8], F32)
        w4b_sb = region(W1B, 0, 2048)
        wr4_sb = sb("wr4_sb", [1, TT], F32)
        rms_sb = sb("rms_sb", [128, 512], F32)
        aofm_sb = sb("aofm_sb", [128, 4096], BF16)
        mean_sb = region(aofm_sb, 0, 3072)
        wo_st = [sb("wo_st0", [128, 1024], BF16)]
        xt_t = sb("xt_t", [128, 1024], F32)
        ao_tm = sb("ao_tm", [128, 1024], F32)
        lnsq = sb("lnsq", [128, 1024], F32)
        x2_t = lnsq
        wstg2 = ao_tm
        bt_f32 = regt(lnsq, 0, 4, 0, 512)
        stat_sb = sb("stat_sb", [128, 16], F32)
        m3_sb = sb("m3_sb", [128, 64], F32)
        r3_sb = sb("r3_sb", [128, 64], F32)
        gmax_sb = sb("gmax_sb", [128, 4], F32)
        ccst_sb = sb("ccst_sb", [128, 128], F32)
        sin_sb = sb("sin_sb", [128, 4], F32)
        sw1_sb = sb("sw1_sb", [128, 32], F32)
        a1_sb = sb("a1_sb", [128, 32], F32)
        bq1_sb = sb("bq1_sb", [128, 32], F32)
        sw2_sb = sb("sw2_sb", [128, 8], F32)
        a2_sb = sb("a2_sb", [128, 8], F32)
        bq2_sb = sb("bq2_sb", [128, 8], F32)
        tiny_sb = sb("tiny_sb", [128, 8], F32)
        tiny2_sb = sb("tiny2_sb", [128, 32], F32)
        xq_tm = sb("xq_tm", [128, 1024], BF16)
        ao_bf = xq_tm
        ht = [rms_sb, None]

        ht[1] = regt(xt_t, 0, 128, 0, 512)
        ps_S = ctx.enter_context(nc.psum_tensor("ps_S", [128, 512], F32))
        ps_a = ctx.enter_context(nc.psum_tensor("ps_a", [128, 512], F32))
        ps_o = ctx.enter_context(nc.psum_tensor("ps_o", [128, 512], F32))
        ps_mm = ctx.enter_context(nc.psum_tensor("ps_mm", [128, 512], F32))
        ps_mm2 = ctx.enter_context(nc.psum_tensor("ps_mm2", [128, 512], F32))
        ps_p1 = ctx.enter_context(nc.psum_tensor("ps_p1", [128, 512], F32))
        ps_p2 = ctx.enter_context(nc.psum_tensor("ps_p2", [128, 512], F32))
        ps_t = ctx.enter_context(nc.psum_tensor("ps_t", [128, 512], F32))

        sems = {n: ctx.enter_context(nc.semaphore(n)) for n in semnames}
        block = ctx.enter_context(nc.Block())
        MG = C["magicc"]

        def barrier():
            evs = [(s, cnt[s]) for s in semnames]
            for eng in ENGINES:
                fn = (lambda e: e.nop()) if eng != "sync" else (lambda e: e.nop())
                step(eng, fn, extra=evs)

        for n in consts:
            lowp = consts[n].shape[0] < 128
            dma(C[n][:, :], CIN[n][:, :], writes=[C[n]], sem="ds" if lowp else "d")
        for n in WS:
            dma(WS[n][:, :], IN[n][:, :], writes=[WS[n]], sem="ds" if n == "fgb2" else "d")
        dma(w_wbT[:, :].rearrange("p (c h) -> p c h", h=4), IN["wbT"].rearrange("(c p) h -> p c h", p=128), writes=[w_wbT], eng="gpsimd")
        dma(w_fg2T[:, :].rearrange("p (c h) -> p c h", h=4), IN["fg2T"].rearrange("(c p) h -> p c h", p=128), writes=[w_fg2T], eng="gpsimd")

        dma(sc1_sb[0:1, 0:1], IN["ltemp"][:, :], writes=[sc1_sb], sem="ds")
        step("scalar", lambda e: e.activation(out=sc1_sb[0:1, 1:2], in_=sc1_sb[0:1, 0:1], func=ACT.Exp),
             reads=[sc1_sb], writes=[sc1_sb])
        step("scalar", lambda e: e.activation(out=sc1_sb[0:1, 2:3], in_=sc1_sb[0:1, 1:2], func=ACT.Ln, bias=1.0),
             reads=[sc1_sb], writes=[sc1_sb])
        step("vector", lambda e: e.tensor_scalar(out=sc1_sb[0:1, 3:4], in0=sc1_sb[0:1, 2:3],
                                                 scalar1=1e-4, scalar2=None, op0=ALU.add),
             reads=[sc1_sb], writes=[sc1_sb])
        step("vector", lambda e: e.reciprocal(out=sc1_sb[0:1, 4:5], in_=sc1_sb[0:1, 3:4]),
             reads=[sc1_sb], writes=[sc1_sb])
        step("tensor", lambda e: e.matmul(ps_t[0:128, 0:1], C["ones1f"][0:1, :], sc1_sb[0:1, 4:5],
                                          start=True, stop=True),
             reads=[sc1_sb, C["ones1f"]], writes=[ps_t])
        step("scalar", lambda e: e.copy(out=itmp_sb[:, :], in_=ps_t[0:128, 0:1]),
             reads=[ps_t], writes=[itmp_sb])
        step("gpsimd", lambda e: e.memset(ccst_sb[:, :], 0.0), writes=[ccst_sb])
        step("gpsimd", lambda e: e.memset(gmax_sb[:, :], -1e30), writes=[gmax_sb])
        for c in range(NCH):
            step("gpsimd", lambda e, c=c: e.memset(VFM[:, c * PL:c * PL + PAD], 0.0), writes=[VFM])

        convg = sq_sb

        def conv(src, s0raw, dst, d0, wt, wc0, k, width):
            s0 = s0raw - (k - 1)
            def chain(eng, target, t0, taps):
                for jj, j in enumerate(taps):
                    so = s0 + j
                    if jj == 0:
                        step(eng, lambda e, so=so, j=j, target=target, t0=t0: e.tensor_scalar(
                            out=target[:, t0:t0 + width], in0=src[:, so:so + width],
                            scalar1=wt[:, wc0 + j:wc0 + j + 1], scalar2=None, op0=ALU.mult),
                            reads=[src, wt], writes=[target])
                    else:
                        step(eng, lambda e, so=so, j=j, target=target, t0=t0: e.scalar_tensor_tensor(
                            out=target[:, t0:t0 + width], in0=src[:, so:so + width],
                            scalar=wt[:, wc0 + j:wc0 + j + 1], in1=target[:, t0:t0 + width],
                            op0=ALU.mult, op1=ALU.add),
                            reads=[src, wt, target], writes=[target])
            chain("vector", dst, d0, list(range(k)))

        def ln_stats_apply(src_tm, ctr_tm, add_tm, out_tm):
            step("vector", lambda e: e.reduce_sum(out=stat_sb[:, 0:1], in_=src_tm[:, :], axis=AXX),
                 reads=[src_tm], writes=[stat_sb])
            step("scalar", lambda e: e.mul(out=stat_sb[:, 1:2], in_=stat_sb[:, 0:1], mul=-1.0 / 1024),
                 reads=[stat_sb], writes=[stat_sb])
            step("vector", lambda e: e.tensor_scalar(out=ctr_tm[:, :], in0=src_tm[:, :],
                                                     scalar1=stat_sb[:, 1:2], scalar2=None, op0=ALU.add),
                 reads=[src_tm, stat_sb], writes=[ctr_tm])
            step("vector", lambda e: e.tensor_tensor(out=lnsq[:, :], in0=ctr_tm[:, :], in1=ctr_tm[:, :],
                                                     op=ALU.mult),
                 reads=[ctr_tm], writes=[lnsq])
            step("vector", lambda e: e.reduce_sum(out=stat_sb[:, 2:3], in_=lnsq[:, :], axis=AXX),
                 reads=[lnsq], writes=[stat_sb])
            step("scalar", lambda e: e.activation(out=stat_sb[:, 3:4], in_=stat_sb[:, 2:3], func=ACT.Sqrt,
                                                  scale=C["i1024"][:, :], bias=C["eps5"][:, :]),
                 reads=[stat_sb, C["i1024"], C["eps5"]], writes=[stat_sb])
            step("vector", lambda e: e.reciprocal(out=stat_sb[:, 4:5], in_=stat_sb[:, 3:4]),
                 reads=[stat_sb], writes=[stat_sb])
            step("scalar", lambda e: e.copy(out=stat_sb[:, 5:6], in_=stat_sb[:, 4:5]),
                 reads=[stat_sb], writes=[stat_sb])
            step("vector", lambda e: e.scalar_tensor_tensor(out=out_tm[:, :], in0=ctr_tm[:, :],
                                                            scalar=stat_sb[:, 5:6], in1=add_tm[:, :],
                                                            op0=ALU.mult, op1=ALU.add),
                 reads=[ctr_tm, stat_sb, add_tm], writes=[out_tm])

        barrier()
        for sq in range(NSEQ):
            tokb = sq * L

            for t in range(NT128):
                r0 = tokb + 128 * t
                dma(xt_t[:, :], x_in[r0:r0 + 128, :], writes=[xt_t])
                step("vector", lambda e: e.reduce_sum(out=stat_sb[:, 0:1], in_=xt_t[:, :], axis=AXX),
                     reads=[xt_t], writes=[stat_sb])
                step("scalar", lambda e: e.mul(out=stat_sb[:, 1:2], in_=stat_sb[:, 0:1], mul=-1.0 / 1024),
                     reads=[stat_sb], writes=[stat_sb])
                step("vector", lambda e: e.tensor_scalar(out=xt_t[:, :], in0=xt_t[:, :],
                                                         scalar1=stat_sb[:, 1:2], scalar2=None, op0=ALU.add),
                     reads=[xt_t, stat_sb], writes=[xt_t])
                step("vector", lambda e: e.tensor_tensor(out=lnsq[:, :], in0=xt_t[:, :], in1=xt_t[:, :],
                                                         op=ALU.mult),
                     reads=[xt_t], writes=[lnsq])
                step("vector", lambda e: e.reduce_sum(out=stat_sb[:, 2:3], in_=lnsq[:, :], axis=AXX),
                     reads=[lnsq], writes=[stat_sb])
                step("scalar", lambda e: e.activation(out=stat_sb[:, 3:4], in_=stat_sb[:, 2:3], func=ACT.Sqrt,
                                                      scale=C["i1024"][:, :], bias=C["eps5"][:, :]),
                     reads=[stat_sb, C["i1024"], C["eps5"]], writes=[stat_sb])
                step("vector", lambda e: e.reciprocal(out=stat_sb[:, 4:5], in_=stat_sb[:, 3:4]),
                     reads=[stat_sb], writes=[stat_sb])
                step("scalar", lambda e: e.copy(out=stat_sb[:, 5:6], in_=stat_sb[:, 4:5]),
                     reads=[stat_sb], writes=[stat_sb])
                step("vector", lambda e: e.tensor_scalar(out=xq_tm[:, :], in0=xt_t[:, :],
                                                          scalar1=stat_sb[:, 5:6], scalar2=None,
                                                          op0=ALU.mult),
                     reads=[xt_t, stat_sb], writes=[xq_tm])
                dmat(XLN[:, :].rearrange("p (c tk) -> p c tk", tk=L)[:, :, 128 * t:128 * t + 128],
                     xq_tm[:, :].rearrange("p (c f) -> p c f", f=128),
                     reads=[xq_tm], writes=[XLN])

            for tt in range(NTT):
                col = tt * TT
                def mmb(e, col=col):
                    ins = None
                    for kc in range(NCH):
                        ins = e.matmul(ps_t[0:4, 0:TT], w_wbT[:, 4 * kc:4 * kc + 4],
                                       XLN[:, kc * L + col:kc * L + col + TT],
                                       start=(kc == 0), stop=(kc == NCH - 1))
                    return ins
                step("tensor", mmb, reads=[XLN, w_wbT], writes=[ps_t])
                step("scalar", lambda e, col=col: e.activation(out=bt_f32[0:4, 0:TT], in_=ps_t[0:4, 0:TT],
                                                               func=ACT.Sigmoid),
                     reads=[ps_t], writes=[bt_f32])
                for bi in range(4):
                    t = 4 * tt + bi
                    step("tensor", lambda e, bi=bi: e.transpose(ps_t[0:128, 4:8],
                                                                bt_f32[0:4, 128 * bi:128 * bi + 128],
                                                                C["idf"][0:4, 0:4]),
                         reads=[bt_f32, C["idf"]], writes=[ps_t])
                    step("scalar", lambda e, t=t: e.copy(out=bpos_sb[:, 4 * t:4 * t + 4], in_=ps_t[0:128, 4:8]),
                         reads=[ps_t], writes=[bpos_sb])

            dma(AR[:, 0:8192].rearrange("p (kc m) -> p kc m", m=1024),
                IN["wvT"].rearrange("(kc p) m -> p kc m", p=128), writes=[AR], eng="gpsimd")
            for c in range(NCH):
                pre_c = pre_cs[c % 2]
                for tt in range(NTT):
                    col = tt * TT
                    pp = ps_mm if tt % 2 == 0 else ps_mm2
                    def mmv(e, c=c, col=col, pp=pp):
                        ins = None
                        for kc in range(NCH):
                            ins = e.matmul(pp[:, 0:TT], AR[:, kc * 1024 + 128 * c:kc * 1024 + 128 * c + 128],
                                           XLN[:, kc * L + col:kc * L + col + TT],
                                           start=(kc == 0), stop=(kc == NCH - 1))
                        return ins
                    step("tensor", mmv, reads=[XLN, AR], writes=[pp])
                    step("scalar", lambda e, col=col, pp=pp, pre_c=pre_c: e.copy(
                        out=pre_c[:, PAD + col:PAD + col + TT], in_=pp[:, 0:TT]),
                        reads=[pp], writes=[pre_c])
                step("gpsimd", lambda e: e.memset(pre_c[:, 0:PAD], 0.0), writes=[pre_c])
                vo = c * PL + PAD
                conv(pre_c, PAD, VFM, vo, WS["vcw"], 4 * c, 4, L)
                step("scalar", lambda e, vo=vo: e.activation(out=VFM[:, vo:vo + L], in_=VFM[:, vo:vo + L],
                                                             func=ACT.Silu),
                     reads=[VFM], writes=[VFM])

            for h in range(H):
                wq_ap = IN["wqT"].rearrange("(kc p) m -> p kc m", p=128)[:, :, 256 * h:256 * h + 256]
                wk_ap = IN["wkT"].rearrange("(kc p) m -> p kc m", p=128)[:, :, 256 * h:256 * h + 256]
                dma(wqk[:, 0:2048].rearrange("p (kc m) -> p kc m", m=256), wq_ap, writes=[wqk], eng="gpsimd")
                dma(wqk[:, 2048:4096].rearrange("p (kc m) -> p kc m", m=256), wk_ap, writes=[wqk], eng="gpsimd")
                for base, dstt, cwn in ((0, q_fm, "qcw"), (2048, k_fm, "kcw")):
                    for cc in range(2):
                        pre_c = pre_cs[cc]
                        for tt in range(NTT):
                            col = tt * TT
                            pp = ps_mm if tt % 2 == 0 else ps_mm2
                            def mmqk(e, base=base, cc=cc, col=col, pp=pp):
                                ins = None
                                for kc in range(NCH):
                                    ins = e.matmul(pp[:, 0:TT],
                                                   wqk[:, base + kc * 256 + 128 * cc:base + kc * 256 + 128 * cc + 128],
                                                   XLN[:, kc * L + col:kc * L + col + TT],
                                                   start=(kc == 0), stop=(kc == NCH - 1))
                                return ins
                            step("tensor", mmqk, reads=[XLN, wqk], writes=[pp])
                            step("scalar", lambda e, col=col, pp=pp, pre_c=pre_c: e.copy(
                                out=pre_c[:, PAD + col:PAD + col + TT], in_=pp[:, 0:TT]),
                                reads=[pp], writes=[pre_c])
                        step("gpsimd", lambda e: e.memset(pre_c[:, 0:PAD], 0.0), writes=[pre_c])
                        o = cc * PL + PAD
                        conv(pre_c, PAD, dstt, o, WS[cwn], 4 * (2 * h + cc), 4, L)
                        step("scalar", lambda e, o=o, dstt=dstt: e.activation(
                            out=dstt[:, o:o + L], in_=dstt[:, o:o + L], func=ACT.Silu),
                            reads=[dstt], writes=[dstt])

                for src, isk in ((q_fm, False), (k_fm, True)):
                    for tt in range(NTT):
                        col = tt * TT
                        for cc in range(2):
                            step("vector", lambda e, src=src, cc=cc, col=col: e.tensor_tensor(
                                out=sq_sb[:, cc * 512:cc * 512 + TT],
                                in0=src[:, cc * PL + PAD + col:cc * PL + PAD + col + TT],
                                in1=src[:, cc * PL + PAD + col:cc * PL + PAD + col + TT], op=ALU.mult),
                                reads=[src], writes=[sq_sb])
                        def mmn(e):
                            e.matmul(ps_t[:, 0:TT], C["onesb"][:, :], sq_sb[:, 0:TT], start=True, stop=False)
                            return e.matmul(ps_t[:, 0:TT], C["onesb"][:, :], sq_sb[:, 512:512 + TT],
                                            start=False, stop=True)
                        step("tensor", mmn, reads=[sq_sb, C["onesb"]], writes=[ps_t])
                        step("vector", lambda e: e.tensor_scalar(out=rms_sb[:, 0:TT], in0=ps_t[:, 0:TT],
                                                                 scalar1=1e-12, scalar2=None, op0=ALU.add),
                             reads=[ps_t], writes=[rms_sb])
                        step("vector", lambda e: e.reciprocal(out=rms_sb[:, 0:TT], in_=rms_sb[:, 0:TT]),
                             reads=[rms_sb], writes=[rms_sb])
                        step("scalar", lambda e: e.activation(out=rms_sb[:, 0:TT], in_=rms_sb[:, 0:TT],
                                                              func=ACT.Sqrt),
                             reads=[rms_sb], writes=[rms_sb])
                        for cc in range(2):
                            so = cc * PL + PAD + col
                            step("vector", lambda e, src=src, so=so: e.tensor_tensor(
                                out=src[:, so:so + TT], in0=src[:, so:so + TT], in1=rms_sb[:, 0:TT],
                                op=ALU.mult),
                                reads=[src, rms_sb], writes=[src])

                for cc in range(2):
                    dmat(kpos.rearrange("p (j cf) -> p j cf", cf=256)[:, :, 128 * cc:128 * cc + 128],
                         k_fm[:, cc * PL + PAD:cc * PL + PAD + L].rearrange("p (j f) -> p j f", f=128),
                         reads=[k_fm], writes=[kpos])
                    dmat(vpos.rearrange("p (j cf) -> p j cf", cf=256)[:, :, 128 * cc:128 * cc + 128],
                         VFM[:, (2 * h + cc) * PL + PAD:(2 * h + cc) * PL + PAD + L].rearrange(
                             "p (j f) -> p j f", f=128),
                         reads=[VFM], writes=[vpos])

                for j in range(NCK):
                    t0 = 128 * j
                    def mmg(e, t0=t0):
                        e.matmul(ps_p1[:, 0:128], k_fm[:, PAD + t0:PAD + t0 + 128],
                                 k_fm[:, PAD + t0:PAD + t0 + 128], start=True, stop=False)
                        e.matmul(ps_p1[:, 0:128], k_fm[:, PL + PAD + t0:PL + PAD + t0 + 128],
                                 k_fm[:, PL + PAD + t0:PL + PAD + t0 + 128], start=False, stop=True)
                        e.matmul(ps_p1[:, 256:384], k_fm[:, PAD + t0:PAD + t0 + 128],
                                 q_fm[:, PAD + t0:PAD + t0 + 128], start=True, stop=False)
                        return e.matmul(ps_p1[:, 256:384], k_fm[:, PL + PAD + t0:PL + PAD + t0 + 128],
                                        q_fm[:, PL + PAD + t0:PL + PAD + t0 + 128], start=False, stop=True)
                    step("tensor", mmg, reads=[k_fm, q_fm], writes=[ps_p1])
                    def masks(e, j=j, h=h):
                        e.scalar_tensor_tensor(out=xch[0][:, :], in0=ps_p1[:, 0:128],
                                               scalar=bpos_sb[:, 4 * j + h:4 * j + h + 1],
                                               in1=C["mlo"][:, :], op0=ALU.mult, op1=ALU.mult)
                        return e.tensor_tensor(out=at_s[:, 128 * j:128 * j + 128], in0=ps_p1[:, 256:384],
                                               in1=C["mupi"][:, :], op=ALU.mult)
                    step("vector", masks, reads=[ps_p1, C["mlo"], C["mupi"], bpos_sb],
                         writes=[xch[0], at_s])
                    step("tensor", lambda e: e.transpose(ps_p2[:, 384:512], xch[0][:, :], C["idf"][:, :]),
                         reads=[xch[0], C["idf"]], writes=[ps_p2])
                    def b0r0(e):
                        e.tensor_copy(out=bch[0][:, :], in_=ps_p2[:, 384:512])
                        return e.tensor_tensor(out=rch[0][:, :], in0=C["idf"][:, :], in1=bch[0][:, :],
                                               op=ALU.subtract)
                    step("vector", b0r0, reads=[ps_p2, C["idf"]], writes=[bch[0], rch[0]])
                    for k in range(1, 7):
                        xp, bp, rp = xch[(k - 1) % 2], bch[(k - 1) % 2], rch[(k - 1) % 2]
                        xn, bn, rn = xch[k % 2], bch[k % 2], rch[k % 2]
                        last = (k == 6)
                        def mmxb(e, xp=xp, bp=bp, last=last):
                            ins = e.matmul(ps_p2[:, 0:128], bp[:, :], xp[:, :], start=True, stop=True)
                            if not last:
                                ins = e.matmul(ps_p2[:, 128:256], xp[:, :], bp[:, :], start=True, stop=True)
                            return ins
                        step("tensor", mmxb, reads=[xp, bp], writes=[ps_p2])
                        def evxb(e, xn=xn, bn=bn, last=last):
                            ins = e.tensor_copy(out=xn[:, :], in_=ps_p2[:, 0:128])
                            if not last:
                                ins = e.tensor_copy(out=bn[:, :], in_=ps_p2[:, 128:256])
                            return ins
                        step("vector", evxb, reads=[ps_p2], writes=[xn, bn])
                        def mmr(e, xn=xn, rp=rp):
                            e.matmul(ps_p2[:, 256:384], C["idf"][:, :], rp[:, :], start=True, stop=False)
                            return e.matmul(ps_p2[:, 256:384], xn[:, :], rp[:, :], start=False, stop=True)
                        step("tensor", mmr, reads=[xn, rp, C["idf"]], writes=[ps_p2])
                        if last:
                            step("vector", lambda e, j=j: e.tensor_copy(
                                out=rt_s[:, 128 * j:128 * j + 128], in_=ps_p2[:, 256:384]),
                                reads=[ps_p2], writes=[rt_s])
                        else:
                            step("vector", lambda e, rn=rn: e.tensor_copy(out=rn[:, :], in_=ps_p2[:, 256:384]),
                                 reads=[ps_p2], writes=[rn])

                step("gpsimd", lambda e: e.memset(s_sb[:, :], 0.0), writes=[s_sb])
                for j in range(NCK):
                    t0 = 128 * j
                    def mmz(e, t0=t0):
                        e.matmul(ps_a[:, 0:256], k_fm[:, PAD + t0:PAD + t0 + 128], s_sb[:, 0:256],
                                 start=True, stop=False)
                        return e.matmul(ps_a[:, 0:256], k_fm[:, PL + PAD + t0:PL + PAD + t0 + 128],
                                        s_sb[:, 256:512], start=False, stop=True)
                    step("tensor", mmz, reads=[k_fm, s_sb], writes=[ps_a])
                    def zfn(e, j=j, h=h):
                        e.tensor_tensor(out=z_sb[:, :], in0=vpos[:, 256 * j:256 * j + 256],
                                        in1=ps_a[:, 0:256], op=ALU.subtract)
                        return e.tensor_scalar(out=z_sb[:, :], in0=z_sb[:, :],
                                               scalar1=bpos_sb[:, 4 * j + h:4 * j + h + 1],
                                               scalar2=None, op0=ALU.mult)
                    step("vector", zfn, reads=[vpos, bpos_sb, ps_a], writes=[z_sb])
                    step("tensor", lambda e, j=j: e.matmul(ps_a[:, 256:512], rt_s[:, 128 * j:128 * j + 128],
                                                           z_sb[:, :], start=True, stop=True),
                         reads=[rt_s, z_sb], writes=[ps_a])
                    step("vector", lambda e: e.tensor_copy(out=u_sb[:, :], in_=ps_a[:, 256:512]),
                         reads=[ps_a], writes=[u_sb])
                    def mmo(e, t0=t0, j=j):
                        e.matmul(ps_o[:, 0:256], q_fm[:, PAD + t0:PAD + t0 + 128], s_sb[:, 0:256],
                                 start=True, stop=False)
                        e.matmul(ps_o[:, 0:256], q_fm[:, PL + PAD + t0:PL + PAD + t0 + 128], s_sb[:, 256:512],
                                 start=False, stop=False)
                        return e.matmul(ps_o[:, 0:256], at_s[:, 128 * j:128 * j + 128], u_sb[:, :],
                                        start=False, stop=True)
                    step("tensor", mmo, reads=[q_fm, s_sb, at_s, u_sb], writes=[ps_o])
                    def dpev(e, j=j):
                        e.tensor_copy(out=dpos0[:, 128 * j:128 * j + 128], in_=ps_o[:, 0:128])
                        return e.tensor_copy(out=dpos1[:, 128 * j:128 * j + 128], in_=ps_o[:, 128:256])
                    step("vector", dpev, reads=[ps_o], writes=[dpos0, dpos1])
                    def mms(e, j=j):
                        e.matmul(ps_S[:, 0:256], kpos[:, 256 * j:256 * j + 128], u_sb[:, :],
                                 start=(j == 0), stop=(j == NCK - 1))
                        return e.matmul(ps_S[:, 256:512], kpos[:, 256 * j + 128:256 * j + 256], u_sb[:, :],
                                        start=(j == 0), stop=(j == NCK - 1))
                    step("tensor", mms, reads=[kpos, u_sb], writes=[ps_S])
                    step("vector", lambda e: e.tensor_copy(out=s_sb[:, :], in_=ps_S[:, 0:512]),
                         reads=[ps_S], writes=[s_sb])

                for cc, dp in ((0, dpos0), (1, dpos1)):
                    dmat(DLT[:, (2 * h + cc) * L:(2 * h + cc + 1) * L].rearrange("p (j f) -> p j f", f=128),
                         dp[:, 0:2048].rearrange("p (j f) -> p j f", f=128),
                         reads=[dp], writes=[DLT])

            if dbg:
                for c_ in range(NCH):
                    dma(dbg_d[128 * c_:128 * c_ + 128, tokb:tokb + L], DLT[:, c_ * L:(c_ + 1) * L],
                        reads=[DLT], eng="gpsimd")
            dma(AR[:, 0:14336].rearrange("p (kc m) -> p kc m", m=1024),
                IN["fg1T"].rearrange("(kc p) m -> p kc m", p=128), writes=[AR], eng="gpsimd")
            for tt in range(NTT):
                col = tt * TT
                for c in range(NCH):
                    conv(VFM, c * PL + PAD + col, fir_s, c * TT, WS["firS"], 3 * c, 3, TT)
                    conv(VFM, c * PL + PAD + col, fir_l, c * TT, WS["firL"], 31 * c, 31, TT)
                for name, srcten, scol, sw in (("s", fir_s, 0, TT), ("l", fir_l, 0, TT), ("d", DLT, col, L)):
                    base = {"s": 0, "l": 2, "d": 4}[name]
                    for cc in range(2):
                        def mfn(e, srcten=srcten, scol=scol, cc=cc, base=base, sw=sw):
                            e.tensor_tensor(out=sq_sb[:, 0:TT], in0=srcten[:, cc * sw + scol:cc * sw + scol + TT],
                                            in1=srcten[:, (cc + 2) * sw + scol:(cc + 2) * sw + scol + TT],
                                            op=ALU.add)
                            e.tensor_tensor(out=sq_sb[:, 512:1024],
                                            in0=srcten[:, (cc + 4) * sw + scol:(cc + 4) * sw + scol + TT],
                                            in1=srcten[:, (cc + 6) * sw + scol:(cc + 6) * sw + scol + TT],
                                            op=ALU.add)
                            e.tensor_tensor(out=sq_sb[:, 0:TT], in0=sq_sb[:, 0:TT], in1=sq_sb[:, 512:1024],
                                            op=ALU.add)
                            return e.tensor_scalar(out=mean_sb[:, (base + cc) * TT:(base + cc + 1) * TT],
                                                   in0=sq_sb[:, 0:TT], scalar1=0.25, scalar2=None, op0=ALU.mult)
                        step("vector", mfn, reads=[srcten], writes=[mean_sb, sq_sb])

                for mc in range(8):
                    pp = ps_mm if mc % 2 == 0 else ps_mm2
                    def mmg1(e, mc=mc, pp=pp, col=col):
                        ins = None
                        for kc in range(14):
                            if kc < 8:
                                rhs = XLN[:, kc * L + col:kc * L + col + TT]
                            else:
                                rhs = mean_sb[:, (kc - 8) * TT:(kc - 7) * TT]
                            ins = e.matmul(pp[:, 0:TT], AR[:, kc * 1024 + 128 * mc:kc * 1024 + 128 * mc + 128],
                                           rhs, start=(kc == 0), stop=(kc == 13))
                        return ins
                    step("tensor", mmg1, reads=[XLN, mean_sb, AR], writes=[pp])
                    step("scalar", lambda e, mc=mc, pp=pp: e.activation(
                        out=ghid_sb[:, mc * TT:(mc + 1) * TT], in_=pp[:, 0:TT], func=ACT.Gelu,
                        bias=WS["fgb1"][:, mc:mc + 1]),
                        reads=[pp, WS["fgb1"]], writes=[ghid_sb])
                def mmg2(e):
                    ins = None
                    for kc in range(8):
                        ins = e.matmul(ps_t[0:4, 0:TT], w_fg2T[:, 4 * kc:4 * kc + 4],
                                       ghid_sb[:, kc * TT:(kc + 1) * TT], start=(kc == 0), stop=(kc == 7))
                    return ins
                step("tensor", mmg2, reads=[ghid_sb, w_fg2T], writes=[ps_t])
                step("scalar", lambda e: e.activation(out=bt_f32[0:4, 0:TT], in_=ps_t[0:4, 0:TT],
                                                      func=ACT.Identity, bias=WS["fgb2"][0:4, 0:1]),
                     reads=[ps_t, WS["fgb2"]], writes=[bt_f32])
                slots = [xt_t[:, 0:512], xt_t[:, 512:1024], ao_tm[:, 0:512], ao_tm[:, 512:1024]]
                smark = [xt_t, xt_t, ao_tm, ao_tm]
                mxs = lnsq[:, 512:1024]
                ssm = rms_sb[:, 0:512]
                for i in range(4):
                    pp = ps_mm if i % 2 == 0 else ps_mm2
                    step("gpsimd", lambda e, i=i: e.dma_start(
                        out=wr4_sb[0:1, 0:TT], in_=bt_f32[i:i + 1, 0:TT]),
                        reads=[bt_f32], writes=[wr4_sb], sem="gs", amt=16)
                    step("tensor", lambda e, pp=pp: e.matmul(pp[:, 0:TT], C["ones1f"][0:1, :],
                                                             wr4_sb[0:1, 0:TT],
                                                             start=True, stop=True),
                         reads=[wr4_sb, C["ones1f"]], writes=[pp])
                    step("vector", lambda e, i=i, pp=pp: e.tensor_scalar(
                        out=slots[i], in0=pp[:, 0:TT], scalar1=itmp_sb[:, 0:1], scalar2=None,
                        op0=ALU.mult),
                        reads=[pp, itmp_sb], writes=[smark[i]])
                def smx(e):
                    e.tensor_tensor(out=mxs, in0=slots[0], in1=slots[1], op=ALU.max)
                    e.tensor_tensor(out=ssm, in0=slots[2], in1=slots[3], op=ALU.max)
                    e.tensor_tensor(out=mxs, in0=mxs, in1=ssm, op=ALU.max)
                    e.tensor_tensor(out=slots[0], in0=slots[0], in1=mxs, op=ALU.subtract)
                    e.tensor_tensor(out=slots[1], in0=slots[1], in1=mxs, op=ALU.subtract)
                    e.tensor_tensor(out=slots[2], in0=slots[2], in1=mxs, op=ALU.subtract)
                    return e.tensor_tensor(out=slots[3], in0=slots[3], in1=mxs, op=ALU.subtract)
                step("vector", smx, reads=[xt_t, ao_tm], writes=[xt_t, ao_tm, lnsq, rms_sb])
                def sexp(e):
                    ins = None
                    for i in range(4):
                        ins = e.activation(out=slots[i], in_=slots[i], func=ACT.Exp)
                    return ins
                step("scalar", sexp, reads=[xt_t, ao_tm], writes=[xt_t, ao_tm])
                def ssum1(e):
                    e.tensor_tensor(out=ssm, in0=slots[0], in1=slots[1], op=ALU.add)
                    e.tensor_tensor(out=ssm, in0=ssm, in1=slots[2], op=ALU.add)
                    e.tensor_tensor(out=ssm, in0=ssm, in1=slots[3], op=ALU.add)
                    return e.reciprocal(out=mxs, in_=ssm)
                step("vector", ssum1, reads=[xt_t, ao_tm], writes=[rms_sb, lnsq])
                step("scalar", lambda e: e.copy(out=ssm, in_=mxs), reads=[lnsq], writes=[rms_sb])
                def snrm(e):
                    ins = None
                    for i in range(4):
                        ins = e.tensor_tensor(out=slots[i], in0=slots[i], in1=ssm, op=ALU.mult)
                    return ins
                step("vector", snrm, reads=[rms_sb, xt_t, ao_tm], writes=[xt_t, ao_tm])
                def sfloor(e):
                    e.tensor_scalar(out=mxs, in0=slots[3], scalar1=0.05, scalar2=None, op0=ALU.mult)
                    e.tensor_tensor(out=slots[0], in0=slots[0], in1=mxs, op=ALU.max)
                    e.tensor_tensor(out=slots[1], in0=slots[1], in1=mxs, op=ALU.max)
                    e.tensor_tensor(out=ssm, in0=slots[0], in1=slots[1], op=ALU.add)
                    e.tensor_tensor(out=ssm, in0=ssm, in1=slots[2], op=ALU.add)
                    e.tensor_tensor(out=ssm, in0=ssm, in1=slots[3], op=ALU.add)
                    return e.reciprocal(out=mxs, in_=ssm)
                step("vector", sfloor, reads=[xt_t, ao_tm], writes=[xt_t, ao_tm, rms_sb, lnsq])
                step("scalar", lambda e: e.copy(out=ssm, in_=mxs), reads=[lnsq], writes=[rms_sb])
                def sfin(e):
                    ins = None
                    for i in range(4):
                        e.tensor_tensor(out=slots[i], in0=slots[i], in1=ssm, op=ALU.mult)
                        ins = e.tensor_copy(out=w4b_sb[:, i * TT:(i + 1) * TT], in_=slots[i])
                    return ins
                step("vector", sfin, reads=[rms_sb, xt_t, ao_tm], writes=[xt_t, ao_tm, w4b_sb])
                for c in range(NCH):
                    def cfn(e, c=c, col=col):
                        e.tensor_tensor(out=sq_sb[:, 0:TT], in0=fir_s[:, c * TT:(c + 1) * TT],
                                        in1=w4b_sb[:, 0:TT], op=ALU.mult)
                        e.tensor_tensor(out=sq_sb[:, 512:1024], in0=fir_l[:, c * TT:(c + 1) * TT],
                                        in1=w4b_sb[:, TT:2 * TT], op=ALU.mult)
                        e.tensor_tensor(out=sq_sb[:, 0:TT], in0=sq_sb[:, 0:TT], in1=sq_sb[:, 512:1024],
                                        op=ALU.add)
                        e.tensor_tensor(out=sq_sb[:, 512:1024], in0=DLT[:, c * L + col:c * L + col + TT],
                                        in1=w4b_sb[:, 2 * TT:3 * TT], op=ALU.mult)
                        e.tensor_tensor(out=sq_sb[:, 0:TT], in0=sq_sb[:, 0:TT], in1=sq_sb[:, 512:1024],
                                        op=ALU.add)
                        e.tensor_tensor(out=sq_sb[:, 512:1024],
                                        in0=VFM[:, c * PL + PAD + col:c * PL + PAD + col + TT],
                                        in1=w4b_sb[:, 3 * TT:4 * TT], op=ALU.mult)
                        return e.tensor_tensor(out=onrm_sb[:, c * TT:(c + 1) * TT], in0=sq_sb[:, 0:TT],
                                               in1=sq_sb[:, 512:1024], op=ALU.add)
                    step("vector", cfn, reads=[fir_s, fir_l, DLT, VFM, w4b_sb], writes=[onrm_sb, sq_sb])
                for hh in range(H):
                    def sqf(e, hh=hh):
                        e.tensor_tensor(out=sq_sb[:, 0:TT], in0=onrm_sb[:, 2 * hh * TT:(2 * hh + 1) * TT],
                                        in1=onrm_sb[:, 2 * hh * TT:(2 * hh + 1) * TT], op=ALU.mult)
                        return e.tensor_tensor(out=sq_sb[:, 512:1024],
                                               in0=onrm_sb[:, (2 * hh + 1) * TT:(2 * hh + 2) * TT],
                                               in1=onrm_sb[:, (2 * hh + 1) * TT:(2 * hh + 2) * TT], op=ALU.mult)
                    step("vector", sqf, reads=[onrm_sb], writes=[sq_sb])
                    def mmrms(e):
                        e.matmul(ps_t[:, 0:TT], C["onesb"][:, :], sq_sb[:, 0:TT], start=True, stop=False)
                        return e.matmul(ps_t[:, 0:TT], C["onesb"][:, :], sq_sb[:, 512:1024],
                                        start=False, stop=True)
                    step("tensor", mmrms, reads=[sq_sb, C["onesb"]], writes=[ps_t])
                    step("vector", lambda e: e.tensor_scalar(out=rms_sb[:, 0:TT], in0=ps_t[:, 0:TT],
                                                             scalar1=1.0 / 256, scalar2=1e-5,
                                                             op0=ALU.mult, op1=ALU.add),
                         reads=[ps_t], writes=[rms_sb])
                    step("vector", lambda e: e.reciprocal(out=rms_sb[:, 0:TT], in_=rms_sb[:, 0:TT]),
                         reads=[rms_sb], writes=[rms_sb])
                    step("scalar", lambda e: e.activation(out=rms_sb[:, 0:TT], in_=rms_sb[:, 0:TT],
                                                          func=ACT.Sqrt),
                         reads=[rms_sb], writes=[rms_sb])
                    for cc in range(2):
                        c = 2 * hh + cc
                        step("vector", lambda e, c=c, cc=cc: e.scalar_tensor_tensor(
                            out=onrm_sb[:, c * TT:(c + 1) * TT], in0=onrm_sb[:, c * TT:(c + 1) * TT],
                            scalar=WS["gnw"][:, cc:cc + 1], in1=rms_sb[:, 0:TT],
                            op0=ALU.mult, op1=ALU.mult),
                            reads=[onrm_sb, WS["gnw"], rms_sb], writes=[onrm_sb])
                if dbg:
                    tc = tokb + col
                    dma(dbg_w[0:4, tc:tc + TT], bt_f32[0:4, 0:TT], reads=[bt_f32], eng="gpsimd", sem="gz")
                    for c_ in range(NCH):
                        dma(dbg_o[128 * c_:128 * c_ + 128, tc:tc + TT],
                            onrm_sb[:, c_ * TT:(c_ + 1) * TT], reads=[onrm_sb], eng="gpsimd", sem="gz")
                    dma(dbg_g[:, tc:tc + TT], ghid_sb[:, 0:TT], reads=[ghid_sb], eng="gpsimd", sem="gz")
                    dma(dbg_l[0:4, tc:tc + TT], bt_f32[0:4, 0:TT], reads=[bt_f32], eng="gpsimd", sem="gz")
                    dma(dbg_fs[:, tc:tc + TT], fir_s[:, 0:TT], reads=[fir_s], eng="gpsimd", sem="gz")
                    dma(dbg_fl[:, tc:tc + TT], fir_l[:, 0:TT], reads=[fir_l], eng="gpsimd", sem="gz")
                    dma(dbg_w4[:, tc:tc + TT], w4b_sb[:, 0:TT], reads=[w4b_sb], eng="gpsimd", sem="gz")
                for mc in range(8):
                    pp = ps_mm if mc % 2 == 0 else ps_mm2
                    wob = wo_st[0]
                    dma(wob[:, :].rearrange("p (kc m) -> p kc m", m=128),
                        IN["woT"].rearrange("(kc p) m -> p kc m", p=128)[:, :, 128 * mc:128 * mc + 128],
                        writes=[wob], eng="gpsimd", sem="gw")
                    def mmop(e, pp=pp, wob=wob):
                        ins = None
                        for kc in range(NCH):
                            ins = e.matmul(pp[:, 0:TT], wob[:, kc * 128:kc * 128 + 128],
                                           onrm_sb[:, kc * TT:(kc + 1) * TT],
                                           start=(kc == 0), stop=(kc == NCH - 1))
                        return ins
                    step("tensor", mmop, reads=[onrm_sb, wob], writes=[pp])
                    step("scalar", lambda e, mc=mc, pp=pp: e.copy(
                        out=aofm_sb[:, :].rearrange("p (b m) -> p b m", m=1024)[:, :, 128 * mc:128 * mc + 128],
                        in_=pp[:, 0:TT].rearrange("p (b f) -> p b f", f=128)),
                        reads=[pp], writes=[aofm_sb])
                for b in range(4):
                    g128 = sq * NT128 + 4 * tt + b
                    r0 = tokb + col + 128 * b
                    dmat(ao_bf[:, :].rearrange("p (c f) -> p c f", f=128),
                         aofm_sb[:, 1024 * b:1024 * b + 1024].rearrange("p (c f) -> p c f", f=128),
                         reads=[aofm_sb], writes=[ao_bf])
                    dma(xt_t[:, :], x_in[r0:r0 + 128, :], writes=[xt_t])
                    ln_stats_apply(ao_bf, ao_tm, xt_t, x2_t)
                    dma(x2_scr[r0:r0 + 128, :], x2_t[:, :], reads=[x2_t])
                    step("vector", lambda e: e.reduce_sum(out=stat_sb[:, 6:7], in_=x2_t[:, :], axis=AXX),
                         reads=[x2_t], writes=[stat_sb])
                    step("scalar", lambda e, g128=g128: e.mul(out=m3_sb[:, g128:g128 + 1],
                                                              in_=stat_sb[:, 6:7], mul=-1.0 / 1024),
                         reads=[stat_sb], writes=[m3_sb])
                    step("vector", lambda e: e.tensor_tensor(out=ao_tm[:, :], in0=x2_t[:, :], in1=x2_t[:, :],
                                                             op=ALU.mult),
                         reads=[x2_t], writes=[ao_tm])
                    step("vector", lambda e: e.reduce_sum(out=stat_sb[:, 7:8], in_=ao_tm[:, :], axis=AXX),
                         reads=[ao_tm], writes=[stat_sb])
                    step("vector", lambda e, g128=g128: e.tensor_tensor(
                        out=stat_sb[:, 8:9], in0=m3_sb[:, g128:g128 + 1], in1=m3_sb[:, g128:g128 + 1],
                        op=ALU.mult),
                        reads=[m3_sb], writes=[stat_sb])
                    step("scalar", lambda e: e.mul(out=stat_sb[:, 9:10], in_=stat_sb[:, 7:8], mul=1.0 / 1024),
                         reads=[stat_sb], writes=[stat_sb])
                    step("vector", lambda e: e.tensor_tensor(out=stat_sb[:, 10:11], in0=stat_sb[:, 9:10],
                                                             in1=stat_sb[:, 8:9], op=ALU.subtract),
                         reads=[stat_sb], writes=[stat_sb])
                    step("scalar", lambda e: e.activation(out=stat_sb[:, 12:13], in_=stat_sb[:, 10:11],
                                                          func=ACT.Sqrt, bias=C["eps5"][:, :]),
                         reads=[stat_sb, C["eps5"]], writes=[stat_sb])
                    step("vector", lambda e, g128=g128: e.reciprocal(out=r3_sb[:, g128:g128 + 1],
                                                                     in_=stat_sb[:, 12:13]),
                         reads=[stat_sb], writes=[r3_sb])
                    step("vector", lambda e: e.reduce_max(out=stat_sb[:, 13:14], in_=x2_t[:, :], axis=AXX),
                         reads=[x2_t], writes=[stat_sb])
                    step("scalar", lambda e: e.copy(out=stat_sb[:, 14:15], in_=stat_sb[:, 13:14]),
                         reads=[stat_sb], writes=[stat_sb])
                    def lmaxfn(e, g128=g128):
                        e.tensor_tensor(out=stat_sb[:, 15:16], in0=stat_sb[:, 14:15],
                                        in1=m3_sb[:, g128:g128 + 1], op=ALU.add)
                        e.tensor_tensor(out=stat_sb[:, 15:16], in0=stat_sb[:, 15:16],
                                        in1=r3_sb[:, g128:g128 + 1], op=ALU.mult)
                        return e.tensor_tensor(out=gmax_sb[:, 0:1], in0=gmax_sb[:, 0:1],
                                               in1=stat_sb[:, 15:16], op=ALU.max)
                    step("vector", lmaxfn, reads=[stat_sb, m3_sb, r3_sb, gmax_sb],
                         writes=[stat_sb, gmax_sb])

        barrier()
        def cross_max_allreduce(srccol, dstcol):
            step("tensor", lambda e: e.transpose(ps_t[0:1, 0:128], gmax_sb[:, srccol:srccol + 1],
                                                 C["idf"][:, :]),
                 reads=[gmax_sb, C["idf"]], writes=[ps_t])
            step("vector", lambda e: e.tensor_copy(out=rms_sb[0:1, 0:128], in_=ps_t[0:1, 0:128]),
                 reads=[ps_t], writes=[rms_sb])
            step("vector", lambda e: e.reduce_max(out=ccst_sb[0:1, 0:1], in_=rms_sb[0:1, 0:128], axis=AXX),
                 reads=[rms_sb], writes=[ccst_sb])
            dma(cc_in[:, :], ccst_sb[:, :], reads=[ccst_sb], eng="gpsimd")
            ev = step("gpsimd", lambda e: e.collective_compute(
                "AllReduce", ALU.max, replica_groups=[[0, 1, 2, 3, 4, 5, 6, 7]],
                ins=[cc_in.opt()], outs=[cc_out.opt()]), sem="g", amt=1,
                extra=[("gd", cnt["gd"])])
            dma(ccst_sb[0:1, 0:1], cc_out[0:1, 0:1], writes=[ccst_sb], eng="gpsimd", extra=[ev])
            step("tensor", lambda e: e.matmul(ps_t[0:128, 0:1], C["ones1f"][0:1, :], ccst_sb[0:1, 0:1],
                                              start=True, stop=True),
                 reads=[ccst_sb, C["ones1f"]], writes=[ps_t])
            step("scalar", lambda e, dstcol=dstcol: e.copy(out=sin_sb[:, dstcol:dstcol + 1],
                                                           in_=ps_t[0:128, 0:1]),
                 reads=[ps_t], writes=[sin_sb])

        for fc in range(NFC):
            dma(lnsq[:, :], IN["w1"][128 * fc:128 * fc + 128, :], writes=[lnsq])
            step("vector", lambda e, fc=fc: e.tensor_reduce(out=sw1_sb[:, fc:fc + 1], in_=lnsq[:, :],
                                                            axis=AXX, op=ALU.max, apply_absolute_value=True),
                 reads=[lnsq], writes=[sw1_sb])
            step("scalar", lambda e, fc=fc: e.mul(out=tiny2_sb[:, fc:fc + 1], in_=sw1_sb[:, fc:fc + 1],
                                                  mul=1.0 / 127),
                 reads=[sw1_sb], writes=[tiny2_sb])
            step("vector", lambda e, fc=fc: e.reciprocal(out=tiny_sb[:, 0:1], in_=tiny2_sb[:, fc:fc + 1]),
                 reads=[tiny2_sb], writes=[tiny_sb])
            step("scalar", lambda e: e.activation(out=wstg2[:, :], in_=lnsq[:, :], func=ACT.Identity,
                                                  scale=tiny_sb[:, 0:1], bias=MG[:, :]),
                 reads=[lnsq, tiny_sb, MG], writes=[wstg2])
            step("vector", lambda e: e.tensor_scalar(out=qw_tile[:, 0:1024], in0=wstg2[:, :],
                                                     scalar1=MAGIC, scalar2=None, op0=ALU.subtract),
                 reads=[wstg2], writes=[qw_tile])
            dmat(W1A[:, :].rearrange("p (dc m) -> p dc m", m=4096)[:, :, 128 * fc:128 * fc + 128],
                 qw_tile[:, 0:1024].rearrange("p (dc f) -> p dc f", f=128)[:, 0:2, :],
                 reads=[qw_tile], writes=[W1A])
            dmat(W1B[:, :].rearrange("p (dc m) -> p dc m", m=4096)[:, :, 128 * fc:128 * fc + 128],
                 qw_tile[:, 0:1024].rearrange("p (dc f) -> p dc f", f=128)[:, 2:4, :],
                 reads=[qw_tile], writes=[W1B])
            dmat(DLT[:, :].rearrange("p (dc m) -> p dc m", m=4096)[:, :, 128 * fc:128 * fc + 128],
                 qw_tile[:, 0:1024].rearrange("p (dc f) -> p dc f", f=128)[:, 4:8, :],
                 reads=[qw_tile], writes=[DLT])

        cross_max_allreduce(0, 0)
        step("vector", lambda e: e.tensor_scalar(out=sin_sb[:, 0:1], in0=sin_sb[:, 0:1],
                                                 scalar1=1e-8, scalar2=1.0 / 255,
                                                 op0=ALU.max, op1=ALU.mult),
             reads=[sin_sb], writes=[sin_sb])
        step("scalar", lambda e: e.copy(out=sin_sb[:, 0:1], in_=sin_sb[:, 0:1]),
             reads=[sin_sb], writes=[sin_sb])
        step("vector", lambda e: e.reciprocal(out=sin_sb[:, 1:2], in_=sin_sb[:, 0:1]),
             reads=[sin_sb], writes=[sin_sb])
        step("vector", lambda e: e.tensor_scalar(out=a1_sb[:, :], in0=tiny2_sb[:, :],
                                                 scalar1=sin_sb[:, 0:1], scalar2=None, op0=ALU.mult),
             reads=[tiny2_sb, sin_sb], writes=[a1_sb])
        step("vector", lambda e: e.reciprocal(out=tiny_sb[:, 6:7] if False else bq1_sb[:, :], in_=a1_sb[:, :]),
             reads=[a1_sb], writes=[bq1_sb])
        step("scalar", lambda e: e.copy(out=bq1_sb[:, :], in_=bq1_sb[:, :]),
             reads=[bq1_sb], writes=[bq1_sb])
        step("vector", lambda e: e.tensor_tensor(out=bq1_sb[:, :], in0=WS["b1"][:, :], in1=bq1_sb[:, :],
                                                 op=ALU.mult),
             reads=[WS["b1"], bq1_sb], writes=[bq1_sb])
        step("scalar", lambda e: e.activation(out=bq1_sb[:, :], in_=bq1_sb[:, :], func=ACT.Identity,
                                              bias=MG[:, :]),
             reads=[bq1_sb, MG], writes=[bq1_sb])
        step("vector", lambda e: e.tensor_scalar(out=bq1_sb[:, :], in0=bq1_sb[:, :], scalar1=MAGIC,
                                                 scalar2=None, op0=ALU.subtract),
             reads=[bq1_sb], writes=[bq1_sb])
        step("vector", lambda e: e.tensor_tensor(out=bq1_sb[:, :], in0=bq1_sb[:, :], in1=a1_sb[:, :],
                                                 op=ALU.mult),
             reads=[bq1_sb, a1_sb], writes=[bq1_sb])

        def w1_lhsT(dc, fc):
            if dc < 2:
                return W1A[:, dc * 4096 + 128 * fc:dc * 4096 + 128 * fc + 128]
            if dc < 4:
                return W1B[:, (dc - 2) * 4096 + 128 * fc:(dc - 2) * 4096 + 128 * fc + 128]
            return DLT[:, (dc - 4) * 4096 + 128 * fc:(dc - 4) * 4096 + 128 * fc + 128]

        for gt in range(TOK // TT):
            col0 = gt * TT
            for b in range(4):
                r0 = col0 + 128 * b
                g128 = r0 // 128
                dma(xt_t[:, :], x2_scr[r0:r0 + 128, :], writes=[xt_t])
                step("vector", lambda e, g128=g128: e.tensor_scalar(
                    out=x2_t[:, :], in0=xt_t[:, :], scalar1=m3_sb[:, g128:g128 + 1],
                    scalar2=r3_sb[:, g128:g128 + 1], op0=ALU.add, op1=ALU.mult),
                    reads=[xt_t, m3_sb, r3_sb], writes=[x2_t])
                step("scalar", lambda e: e.activation(out=x2_t[:, :], in_=x2_t[:, :], func=ACT.Identity,
                                                      scale=sin_sb[:, 1:2], bias=MG[:, :]),
                     reads=[x2_t, sin_sb, MG], writes=[x2_t])
                step("vector", lambda e: e.tensor_scalar(out=xq_tm[:, :], in0=x2_t[:, :],
                                                         scalar1=MAGIC, scalar2=0.0,
                                                         op0=ALU.subtract, op1=ALU.max),
                     reads=[x2_t], writes=[xq_tm])
                dmat(xq_fm.rearrange("p (c tk) -> p c tk", tk=TT)[:, :, 128 * b:128 * b + 128],
                     xq_tm[:, :].rearrange("p (c f) -> p c f", f=128),
                     reads=[xq_tm], writes=[xq_fm])
            for fc in range(NFC):
                pp = ps_mm if fc % 2 == 0 else ps_mm2
                hb = ht[fc % 2]
                def mmf1(e, fc=fc, pp=pp):
                    ins = None
                    for dc in range(8):
                        ins = e.matmul(pp[:, 0:TT], w1_lhsT(dc, fc), xq_fm[:, dc * TT:(dc + 1) * TT],
                                       start=(dc == 0), stop=(dc == 7))
                    return ins
                step("tensor", mmf1, reads=[xq_fm, W1A, W1B, DLT], writes=[pp])
                step("scalar", lambda e, fc=fc, pp=pp, hb=hb: e.activation(
                    out=hb[:, :], in_=pp[:, 0:TT], func=ACT.Relu,
                    scale=a1_sb[:, fc:fc + 1], bias=bq1_sb[:, fc:fc + 1]),
                    reads=[pp, a1_sb, bq1_sb], writes=[hb])
                step("vector", lambda e, hb=hb: e.reduce_max(out=stat_sb[:, 13:14], in_=hb[:, :], axis=AXX),
                     reads=[hb], writes=[stat_sb])
                step("scalar", lambda e: e.copy(out=stat_sb[:, 15:16], in_=stat_sb[:, 13:14]),
                     reads=[stat_sb], writes=[stat_sb])
                step("vector", lambda e: e.tensor_tensor(out=gmax_sb[:, 1:2], in0=gmax_sb[:, 1:2],
                                                         in1=stat_sb[:, 15:16], op=ALU.max),
                     reads=[stat_sb, gmax_sb], writes=[gmax_sb])
                dma(h_scr[fc, :, col0:col0 + TT], hb[:, :], reads=[hb])

        for dc in range(8):
            for qq in range(4):
                dma(lnsq[:, :], IN["w2"][128 * dc:128 * dc + 128, 1024 * qq:1024 * qq + 1024], writes=[lnsq])
                step("vector", lambda e: e.tensor_reduce(out=tiny_sb[:, 4:5], in_=lnsq[:, :],
                                                          axis=AXX, op=ALU.max, apply_absolute_value=True),
                    reads=[lnsq], writes=[tiny_sb])
                step("scalar", lambda e: e.copy(out=tiny_sb[:, 5:6], in_=tiny_sb[:, 4:5]),
                     reads=[tiny_sb], writes=[tiny_sb])
                if qq == 0:
                    step("vector", lambda e, dc=dc: e.tensor_copy(out=sw2_sb[:, dc:dc + 1], in_=tiny_sb[:, 5:6]),
                         reads=[tiny_sb], writes=[sw2_sb])
                else:
                    step("vector", lambda e, dc=dc: e.tensor_tensor(out=sw2_sb[:, dc:dc + 1],
                                                                    in0=sw2_sb[:, dc:dc + 1],
                                                                    in1=tiny_sb[:, 5:6], op=ALU.max),
                         reads=[tiny_sb, sw2_sb], writes=[sw2_sb])
            step("scalar", lambda e, dc=dc: e.mul(out=a2_sb[:, dc:dc + 1], in_=sw2_sb[:, dc:dc + 1],
                                                  mul=1.0 / 127),
                 reads=[sw2_sb], writes=[a2_sb])
            step("vector", lambda e, dc=dc: e.reciprocal(out=tiny_sb[:, 0:1], in_=a2_sb[:, dc:dc + 1]),
                 reads=[a2_sb], writes=[tiny_sb])
            for qq in range(4):
                dma(lnsq[:, :], IN["w2"][128 * dc:128 * dc + 128, 1024 * qq:1024 * qq + 1024], writes=[lnsq])
                step("scalar", lambda e: e.activation(out=wstg2[:, :], in_=lnsq[:, :], func=ACT.Identity,
                                                      scale=tiny_sb[:, 0:1], bias=MG[:, :]),
                     reads=[lnsq, tiny_sb, MG], writes=[wstg2])
                step("vector", lambda e: e.tensor_scalar(out=qw_tile[:, 0:1024], in0=wstg2[:, :],
                                                         scalar1=MAGIC, scalar2=None, op0=ALU.subtract),
                     reads=[wstg2], writes=[qw_tile])
                fclo = 8 * qq
                ins_ap = qw_tile[:, 0:1024].rearrange("p (fc f) -> p fc f", f=128)
                if fclo + 8 <= 16:
                    dmat(VFM[:, 0:16384].rearrange("p (fc m) -> p fc m", m=1024)[:, fclo:fclo + 8,
                         128 * dc:128 * dc + 128], ins_ap, reads=[qw_tile], writes=[VFM])
                elif fclo >= 16 and fclo + 8 <= 30:
                    dmat(AR[:, :].rearrange("p (fc m) -> p fc m", m=1024)[:, fclo - 16:fclo - 8,
                         128 * dc:128 * dc + 128], ins_ap, reads=[qw_tile], writes=[AR])
                else:
                    dmat(AR[:, :].rearrange("p (fc m) -> p fc m", m=1024)[:, 8:14,
                         128 * dc:128 * dc + 128], ins_ap[:, 0:6, :], reads=[qw_tile], writes=[AR])
                    dmat(w2qC.rearrange("p (fc m) -> p fc m", m=1024)[:, 0:2,
                         128 * dc:128 * dc + 128], ins_ap[:, 6:8, :], reads=[qw_tile], writes=[w2qC])

        cross_max_allreduce(1, 2)
        step("vector", lambda e: e.tensor_scalar(out=sin_sb[:, 2:3], in0=sin_sb[:, 2:3],
                                                 scalar1=1e-8, scalar2=1.0 / 255,
                                                 op0=ALU.max, op1=ALU.mult),
             reads=[sin_sb], writes=[sin_sb])
        step("scalar", lambda e: e.copy(out=sin_sb[:, 2:3], in_=sin_sb[:, 2:3]),
             reads=[sin_sb], writes=[sin_sb])
        step("vector", lambda e: e.reciprocal(out=sin_sb[:, 3:4], in_=sin_sb[:, 2:3]),
             reads=[sin_sb], writes=[sin_sb])
        step("vector", lambda e: e.tensor_scalar(out=a2_sb[:, :], in0=a2_sb[:, :],
                                                 scalar1=sin_sb[:, 2:3], scalar2=None, op0=ALU.mult),
             reads=[a2_sb, sin_sb], writes=[a2_sb])
        step("vector", lambda e: e.reciprocal(out=tiny_sb[:, 6:7] if False else bq2_sb[:, :], in_=a2_sb[:, :]),
             reads=[a2_sb], writes=[bq2_sb])
        step("scalar", lambda e: e.copy(out=bq2_sb[:, :], in_=bq2_sb[:, :]),
             reads=[bq2_sb], writes=[bq2_sb])
        step("vector", lambda e: e.tensor_tensor(out=bq2_sb[:, :], in0=WS["b2"][:, :], in1=bq2_sb[:, :],
                                                 op=ALU.mult),
             reads=[WS["b2"], bq2_sb], writes=[bq2_sb])
        step("scalar", lambda e: e.activation(out=bq2_sb[:, :], in_=bq2_sb[:, :], func=ACT.Identity,
                                              bias=MG[:, :]),
             reads=[bq2_sb, MG], writes=[bq2_sb])
        step("vector", lambda e: e.tensor_scalar(out=bq2_sb[:, :], in0=bq2_sb[:, :], scalar1=MAGIC,
                                                 scalar2=None, op0=ALU.subtract),
             reads=[bq2_sb], writes=[bq2_sb])
        step("vector", lambda e: e.tensor_tensor(out=bq2_sb[:, :], in0=bq2_sb[:, :], in1=a2_sb[:, :],
                                                 op=ALU.mult),
             reads=[bq2_sb, a2_sb], writes=[bq2_sb])

        def w2_lhsT(fc, dc):
            if fc < 16:
                return VFM[:, fc * 1024 + 128 * dc:fc * 1024 + 128 * dc + 128]
            if fc < 30:
                return AR[:, (fc - 16) * 1024 + 128 * dc:(fc - 16) * 1024 + 128 * dc + 128]
            return w2qC[:, (fc - 30) * 1024 + 128 * dc:(fc - 30) * 1024 + 128 * dc + 128]

        for gt in range(TOK // TT):
            col0 = gt * TT
            for fc in range(NFC):
                hb = ht[fc % 2]
                dma(hb[:, :], h_scr[fc, :, col0:col0 + TT], writes=[hb])
                step("scalar", lambda e, hb=hb: e.activation(out=hb[:, :], in_=hb[:, :], func=ACT.Identity,
                                                             scale=sin_sb[:, 3:4], bias=MG[:, :]),
                     reads=[hb, sin_sb, MG], writes=[hb])
                step("vector", lambda e, fc=fc, hb=hb: e.tensor_scalar(
                    out=XLN[:, fc * TT:(fc + 1) * TT], in0=hb[:, :],
                    scalar1=MAGIC, scalar2=None, op0=ALU.subtract),
                    reads=[hb], writes=[XLN])
            for dc in range(8):
                pp = ps_mm if dc % 2 == 0 else ps_mm2
                def mmf2(e, dc=dc, pp=pp):
                    ins = None
                    for fc in range(NFC):
                        ins = e.matmul(pp[:, 0:TT], w2_lhsT(fc, dc), XLN[:, fc * TT:(fc + 1) * TT],
                                       start=(fc == 0), stop=(fc == NFC - 1))
                    return ins
                step("tensor", mmf2, reads=[XLN, VFM, AR, w2qC], writes=[pp])
                step("scalar", lambda e, dc=dc, pp=pp: e.activation(
                    out=aofm_sb[:, :].rearrange("p (b m) -> p b m", m=1024)[:, :, 128 * dc:128 * dc + 128],
                    in_=pp[:, 0:TT].rearrange("p (b f) -> p b f", f=128), func=ACT.Identity,
                    scale=a2_sb[:, dc:dc + 1], bias=bq2_sb[:, dc:dc + 1]),
                    reads=[pp, a2_sb, bq2_sb], writes=[aofm_sb])
            for b in range(4):
                r0 = col0 + 128 * b
                dmat(ao_bf[:, :].rearrange("p (c f) -> p c f", f=128),
                     aofm_sb[:, 1024 * b:1024 * b + 1024].rearrange("p (c f) -> p c f", f=128),
                     reads=[aofm_sb], writes=[ao_bf])
                dma(xt_t[:, :], x2_scr[r0:r0 + 128, :], writes=[xt_t])
                ln_stats_apply(ao_bf, ao_tm, xt_t, x2_t)
                step("sync", lambda e, r0=r0: e.dma_start(out=y_out[r0:r0 + 128, :], in_=x2_t[:, :]),
                     reads=[x2_t], sem="st", amt=16)

        steps["sync"].append(([("st", cnt["st"])], None, None))
        import sys
        print("SEM COUNTS:", cnt, file=sys.stderr)
        print("STEPS:", {k: len(v) for k, v in steps.items()}, file=sys.stderr)

        def replay(eng, engine_obj):
            seen = {s: 0 for s in semnames}
            for waits, fn, inc in steps[eng]:
                for smn, val in waits:
                    if val > seen[smn]:
                        engine_obj.wait_ge(sems[smn], val)
                        seen[smn] = val
                if fn is not None:
                    ins = fn(engine_obj)
                    if inc is not None:
                        ins.then_inc(sems[inc[0]], inc[1])

        @block.sync
        def _(sync):
            replay("sync", sync)

        @block.tensor
        def _(tensor):
            replay("tensor", tensor)

        @block.vector
        def _(vector):
            replay("vector", vector)

        @block.scalar
        def _(scalar):
            replay("scalar", scalar)

        @block.gpsimd
        def _(gpsimd):
            replay("gpsimd", gpsimd)

    return nc


_NC_CACHE = [None]


def kernel(**kw):
    g = lambda n: np.ascontiguousarray(np.asarray(kw[n], dtype=np.float32))
    x = g("L0_attn_hidden_states")
    lay = lambda a: np.ascontiguousarray(a)
    cwtile = lambda w: lay(w.reshape(8, 128, w.shape[-1]).transpose(1, 0, 2).reshape(128, -1))
    weights = {
        "wqT": lay(g("L0_attn_q_proj_w").T), "wkT": lay(g("L0_attn_k_proj_w").T),
        "wvT": lay(g("L0_attn_v_proj_w").T), "wbT": lay(g("L0_attn_b_proj_w").T),
        "qcw": cwtile(g("L0_attn_q_conv_w")), "kcw": cwtile(g("L0_attn_k_conv_w")),
        "vcw": cwtile(g("L0_attn_v_conv_w")),
        "firS": cwtile(g("L0_attn_fir_short_w").reshape(1024, 3)),
        "firL": cwtile(g("L0_attn_fir_long_w").reshape(1024, 31)),
        "fg1T": lay(g("L0_attn_fg_w1").T), "fgb1": lay(g("L0_attn_fg_b1").reshape(8, 128).T),
        "fg2T": lay(g("L0_attn_fg_w2").T), "fgb2": lay(g("L0_attn_fg_b2").reshape(4, 1)),
        "ltemp": lay(g("L0_attn_logit_temp").reshape(1, 1)),
        "gnw": lay(g("L0_attn_o_norm_w").reshape(2, 128).T),
        "woT": lay(g("L0_attn_o_proj_w").T),
        "w1": g("L0_ffn_w1"), "b1": lay(g("L0_ffn_b1").reshape(32, 128).T),
        "w2": g("L0_ffn_w2"), "b2": lay(g("L0_ffn_b2").reshape(8, 128).T),
    }
    if _NC_CACHE[0] is None:
        _NC_CACHE[0] = build()
    nc = _NC_CACHE[0]
    in_maps = []
    for c in range(8):
        m = {"x": lay(x[4 * c:4 * c + 4].reshape(TOK, D))}
        m.update(weights)
        in_maps.append(m)
    res = run_bass_kernel_spmd(nc, in_maps, core_ids=list(range(8)))
    out = np.concatenate([np.asarray(res.results[c]["y"]).reshape(4, L, D) for c in range(8)], axis=0)
    return out.astype(np.float32)
